# version 17
# baseline (speedup 1.0000x reference)
import numpy as np
import ml_dtypes
import concourse.bass as bass
import concourse.mybir as mybir
from concourse.bass_utils import run_bass_kernel_spmd

F32 = mybir.dt.float32
BF16 = mybir.dt.bfloat16
I32 = mybir.dt.int32
I8 = mybir.dt.int8
AF = mybir.ActivationFunctionType
ALU = mybir.AluOpType
AX = mybir.AxisListType
BF = ml_dtypes.bfloat16

T, D, H, K = 4096, 1024, 16, 64
C, NCH, GSZ = 128, 32, 4
NG = NCH // GSZ
FFN = 4096
KV = K + 1
MR = 12582912.0
LOG1EM8 = float(np.log(np.float32(1e-8)))
INV_D = 1.0 / D


def ceil_div(a, b):
    return (a + b - 1) // b


class Prog:
    ENGINES = ("tensor", "vector", "scalar", "gpsimd", "sync")

    def __init__(self, nc, ctx):
        self.nc = nc
        self.prog = {e: [] for e in self.ENGINES}
        self.sems = {}
        self.cnt = {}
        self.ctx = ctx

    def _sem(self, name):
        if name not in self.sems:
            self.sems[name] = self.ctx.enter_context(self.nc.semaphore(name))
            self.cnt[name] = 0
        return self.sems[name]

    def wait(self, engine, name, val=None):
        s = self._sem(name)
        v = self.cnt[name] if val is None else val
        if v > 0:
            self.prog[engine].append(lambda eng, s=s, v=v: eng.wait_ge(s, v))

    def op(self, engine, fn, inc=None, dma=False):
        if inc is None:
            self.prog[engine].append(fn)
            return None
        s = self._sem(inc)
        a = 16 if dma else 1
        self.cnt[inc] += a
        self.prog[engine].append(lambda eng, s=s, a=a, fn=fn: fn(eng).then_inc(s, a))
        return self.cnt[inc]

    def val(self, name):
        self._sem(name)
        return self.cnt[name]

    def barrier(self, name):
        s = self._sem(name)
        base = self.cnt[name]
        self.cnt[name] += len(self.ENGINES)
        for e in self.ENGINES:
            if e == "tensor":
                self.prog[e].append(lambda eng, s=s: eng.nop().then_inc(s, 1))
            elif e in ("vector", "gpsimd"):
                self.prog[e].append(lambda eng, s=s: eng.engine_nop().then_inc(s, 1)
                                    if hasattr(eng, "engine_nop") else eng.nop().then_inc(s, 1))
            else:
                self.prog[e].append(lambda eng, s=s: eng.nop().then_inc(s, 1))
        for e in self.ENGINES:
            self.prog[e].append(lambda eng, s=s, v=base + len(self.ENGINES):
                                eng.wait_ge(s, v))

    def finalize(self, block):
        for engine in self.ENGINES:
            ops = list(self.prog[engine])
            def body(eng, ops=ops):
                for f in ops:
                    f(eng)
            getattr(block, engine)(body)


def build_program(debug=False, stage=99, nch1a=NCH, sub3=9, ng1c=NG):
    nc = bass.Bass("TRN2", target_bir_lowering=False)
    from contextlib import ExitStack
    ctx = ExitStack()

    def IN(name, shape, dt=F32):
        return nc.dram_tensor(name, list(shape), dt, kind="ExternalInput").ap()

    x_h = IN("x", (T, D))
    qw_h, kw_h, vw_h, ow_h = (IN(n, (D, D)) for n in ("qw", "kw", "vw", "ow"))
    decw_h = IN("decw", (D, H))
    f1w_h = IN("f1w", (D, FFN))
    f2w_h = IN("f2w", (FFN, D))
    wq_h = IN("wq", (D, D), I8)
    qb_h, kb_h, vb_h = IN("qb", (1, D)), IN("kb", (1, D)), IN("vb", (1, D))
    decb_h = IN("decb", (1, H))
    f1b_h = IN("f1b", (1, FFN))
    ln1w_h, ln1b_h = IN("ln1w", (128, 8)), IN("ln1b", (128, 8))
    ln2wr_h = IN("ln2wr", (1, D))
    ln2w_h, ln2b_h = IN("ln2w", (128, 8)), IN("ln2b", (128, 8))
    obt_h = IN("obt", (128, D))
    f2bt_h = IN("f2bt", (128, D))
    qfbt_h = IN("qfbt", (128, D))
    wst_h = IN("wst", (128, D))
    U_h = IN("Ut", (C, C), BF16)
    negU_h = IN("negUt", (C, C), BF16)
    SL_h = IN("SLt", (C, C), BF16)
    ones128_h = IN("ones128", (C, C), BF16)
    eye16_h = IN("eye16", (16, 16))
    eye1_h = IN("eye1", (1, 1))
    out_h = nc.dram_tensor("out", [T, D], F32, kind="ExternalOutput").ap()

    def SCR(name, shape, dt=F32):
        kind = "ExternalOutput" if debug else "Internal"
        return nc.dram_tensor(name, list(shape), dt, kind=kind).ap()

    xn_sp = SCR("xn_sp", (T, D))
    qffn_sp = SCR("qffn_sp", (T, D))
    xn1_sp = SCR("xn1_sp", (NG, 128, 8 * 512), BF16)
    phiq_sp = SCR("phiq_sp", (NG, 128, 8 * 512), BF16)
    w_sp = SCR("w_sp", (NCH, 128, 8 * KV))
    op_sp = SCR("op_sp", (NCH, 128, 16 * KV))
    xmid_sp = SCR("xmid_sp", (T, D))
    if debug:
        z_dbg = SCR("z_dbg", (128, 512))
        f_dbg = SCR("f_dbg", (128, 16 * NCH))
        e_dbg = SCR("e_dbg", (128, 16 * NCH))
        L_dbg = SCR("L_dbg", (128, 16 * NCH))
        g_dbg = SCR("g_dbg", (128, 16 * NCH))

    S = {}

    def mk_sb(stack):
        def SB(name, shape, dt=F32):
            S[name] = stack.enter_context(nc.sbuf_tensor(name, list(shape), dt))
            return S[name]
        return SB

    perm = ExitStack(); SBp = mk_sb(perm)
    lamS = ExitStack()
    ln1w_s = SBp("ln1w_s", (128, 8))
    ln2w_s = SBp("ln2w_s", (128, 8))
    ln1b16 = SBp("ln1b16", (128, 8), BF16)
    ln2b16 = SBp("ln2b16", (128, 8), BF16)
    lnraw = SBp("lnraw", (128, 16))
    decbe_c = SBp("decbe_c", (16, 1))
    qbe_c = SBp("qbe_c", (128, 8))
    kbe_c = SBp("kbe_c", (128, 8))
    U_s = SBp("U_s", (C, C), BF16)
    negU_s = SBp("negU_s", (C, C), BF16)
    SL_s = SBp("SL_s", (C, C), BF16)
    ones128_s = SBp("ones128_s", (C, C), BF16)
    eye16_s = SBp("eye16_s", (16, 16))
    eye1_s = SBp("eye1_s", (1, 1))
    ones1f = SBp("ones1f", (1, 128))
    st2 = [SBp(f"st{i}", (128, 16)) for i in range(2)]
    scr = SBp("scr", (128, D))
    dumV = SBp("dumV", (1, 8))
    dumS = SBp("dumS", (1, 8))
    dumG = SBp("dumG", (1, 8))
    SBl = mk_sb(lamS)
    z_g = SBl("z_g", (16, 512))
    ztm = SBl("ztm", (128, 16 * NCH))
    g_f32 = SBl("g_f32", (128, 16 * NCH))
    gh_tm = SBl("gh_tm", (128, 16 * NCH), BF16)
    gl_tm = SBl("gl_tm", (128, 16 * NCH), BF16)
    f_all = SBl("f_all", (128, 16 * NCH))
    e_all = SBl("e_all", (128, 16 * NCH))
    Lam_all = SBl("Lam_all", (128, 16 * NCH))
    vb_t = SBl("vb_t", (128, D))

    pA = perm.enter_context(nc.psum_tensor("pA", [128, 1024], F32))
    pB = perm.enter_context(nc.psum_tensor("pB", [128, 1024], F32))
    pC = perm.enter_context(nc.psum_tensor("pC", [128, 1024], F32))
    pD = perm.enter_context(nc.psum_tensor("pD", [128, 1024], F32))
    PREG = {"A": pA, "B": pB, "C": pC, "D": pD}

    p = Prog(nc, ctx)
    nfill = {"A": 0, "B": 0, "C": 0, "D": 0}

    prev_rec = {}

    def _ivals(aps):
        out = []
        for a in aps:
            if a is None or isinstance(a, (int, float)):
                continue
            try:
                name = a.tensor.name
            except Exception:
                continue
            try:
                W = a.tensor.shape[-1]
                dims = a.ap
                off = a.offset
                ok = dims[0][0] == W or dims[0][1] == 1
                for s, cnt in dims[1:]:
                    if cnt > 1 and abs(s) >= W:
                        ok = False
                if not ok or W <= 0:
                    out.append((name, 0, 1 << 40, 0, 1 << 40))
                    continue
                p0, c0 = off // W, off % W
                clo = c0 + sum(min(0, (cnt - 1) * s) for s, cnt in dims[1:])
                chi = c0 + sum(max(0, (cnt - 1) * s) for s, cnt in dims[1:])
                out.append((name, p0, p0 + dims[0][1] - 1, clo, chi))
            except Exception:
                out.append((name, 0, 1 << 40, 0, 1 << 40))
        return out

    def _clash(xs, ys):
        for nx, pl, ph, cl, ch in xs:
            for ny, ql, qh, dl, dh in ys:
                if nx == ny and pl <= qh and ql <= ph and cl <= dh and dl <= ch:
                    return True
        return False

    def hz(engine, outs, ins):
        o, i = _ivals(outs), _ivals(ins)
        wa, ra = prev_rec.setdefault(engine, ([], []))
        if _clash(wa, o + i) or _clash(ra, o):
            p.prog[engine].append(lambda e: e.drain())
            wa, ra = [], []
        prev_rec[engine] = (wa + o, ra + i)

    def hzfence(engine):
        prev_rec[engine] = ([], [])

    def acquire(r):
        p.wait("tensor", r + "e", nfill[r])

    def mm(out, lhsT, rhs, start, stop, inc=None):
        return p.op("tensor", lambda e, o=out, l=lhsT, r=rhs, s=start, t=stop:
                    e.matmul(o, l, r, start=s, stop=t), inc=inc)

    def tp(out, in_, ident, inc=None):
        return p.op("tensor", lambda e, o=out, i=in_, d=ident: e.transpose(o, i, d),
                    inc=inc)

    def ts(engine, out, in0, s1, s2, op0, op1=None, inc=None):
        hz(engine, [out], [in0, s1, s2])
        if op1 is None:
            return p.op(engine, lambda e, o=out, i=in0, a=s1: e.tensor_scalar(
                o, i, a, None, op0), inc=inc)
        return p.op(engine, lambda e, o=out, i=in0, a=s1, b=s2: e.tensor_scalar(
            o, i, a, b, op0, op1), inc=inc)

    def tt(engine, out, a, b, alu, inc=None):
        hz(engine, [out], [a, b])
        return p.op(engine, lambda e, o=out, x=a, y=b: e.tensor_tensor(o, x, y, alu),
                    inc=inc)

    def stt(engine, out, in0, scalar, in1, op0, op1, inc=None):
        hz(engine, [out], [in0, scalar, in1])
        return p.op(engine, lambda e, o=out, i=in0, s=scalar, j=in1:
                    e.scalar_tensor_tensor(o, i, s, j, op0, op1), inc=inc)

    def act(out, in_, func, bias=None, scale=None, accum=None, inc=None):
        hz("scalar", [out, accum], [in_, bias])
        kw = {}
        if bias is not None:
            kw["bias"] = bias
        if scale is not None:
            kw["scale"] = scale
        if accum is not None:
            kw["accum_out"] = accum
        return p.op("scalar", lambda e, o=out, i=in_, f=func, kw=kw:
                    e.activation(o, i, f, **kw), inc=inc)

    def cp(engine, out, in_, inc=None):
        hz(engine, [out], [in_])
        return p.op(engine, lambda e, o=out, i=in_: e.tensor_copy(o, i), inc=inc)

    def red(engine, out, in_, alu, absval=False, inc=None):
        hz(engine, [out], [in_])
        return p.op(engine, lambda e, o=out, i=in_: e.tensor_reduce(
            o, i, AX.X, alu, apply_absolute_value=absval), inc=inc)

    def mset(engine, ap, v, inc=None):
        hz(engine, [ap], [])
        return p.op(engine, lambda e, a=ap, vv=v: e.memset(a, vv), inc=inc)

    def rcp(engine, out, in_, inc=None):
        hz(engine, [out], [in_])
        return p.op(engine, lambda e, o=out, i=in_: e.reciprocal(o, i), inc=inc)

    def bitand(engine, out, in_, mask, inc=None):
        hz(engine, [out], [in_])
        return p.op(engine, lambda e, o=out, i=in_, m=mask: e.tensor_scalar(
            o.bitcast(I32), i.bitcast(I32), m, None, ALU.bitwise_and), inc=inc)

    def dma(dst, src, inc="gdma"):
        return p.op("gpsimd", lambda e, d=dst, s=src: e.dma_start(d, s), inc=inc,
                    dma=True)

    def dmaT(dst, src, inc="tdma"):
        return p.op("sync", lambda e, d=dst, s=src: e.dma_start_transpose(d, s),
                    inc=inc, dma=True)

    def tiles(h, n=8):
        return h[:, :].rearrange("p (t n) -> p t n", t=n)

    SUM, SQ, M, R_, AM, XS, RS, VAR, R2, EPS = 0, 1, 2, 3, 4, 5, 6, 7, 8, 15
    V1, MB1, RR, MB = 11, 12, 13, 14
    PERM8 = [0, 2, 4, 6, 1, 3, 5, 7]
    mk = {}
    def mark(name, c, v):
        mk[(name, c)] = v
    def mval(name, c):
        return mk.get((name, c), 0)

    p1 = ExitStack(); SB1 = mk_sb(p1)
    qw_s = SB1("qw_s", (128, 8 * D), BF16)
    kw_s = SB1("kw_s", (128, 8 * D), BF16)
    vw_s = SB1("vw_s", (128, 8 * D), BF16)
    decw_s = SB1("decw_s", (128, 8 * H), BF16)
    wqt_s = SB1("wqt_s", (128, 8 * D), BF16)
    wst_s = SB1("wst_s", (128, D))
    xbuf = [SB1(f"xbuf{i}", (128, D)) for i in range(2)]
    xnbuf = [SB1(f"xnbuf{i}", (128, D)) for i in range(2)]
    tbuf = [SB1(f"tbuf{i}", (128, D)) for i in range(2)]
    mrc = SB1("mrc", (128, 2))
    xq_tm = [SB1(f"xq_tm{i}", (128, D), BF16) for i in range(2)]
    xq_fm = [SB1(f"xq_fm{i}", (128, 8 * 128), BF16) for i in range(2)]
    xn1_tm = [SB1(f"xn1_tm{i}", (128, D), BF16) for i in range(2)]
    xn1_g = [SB1("xn1_g0", (128, 8 * 512), BF16)] * 2
    qf_buf = [SB1("qf_buf0", (128, D))] * 2
    lam_t1 = SB1("lam_t1", (128, 16 * NCH))
    lam_t2 = SB1("lam_t2", (128, 16 * NCH))
    phiq_g = [SB1("phiq_g0", (128, 8 * 512), BF16)] * 2
    phik_fm = [SB1("phik_fm0", (128, 8 * 512), BF16)] * 2

    prep_sc = ExitStack(); SBq = mk_sb(prep_sc)
    rows_s = SBq("rows_s", (1, 3 * D + 128))
    wstage = SBq("wstage", (128, 8 * D), BF16)

    ld_marks = {}
    def ld(dst, src, tag=None):
        v = dma(dst, src, inc="ld")
        if tag:
            ld_marks[tag] = v

    ld(ln1w_s[:, :], ln1w_h)
    ld(lnraw[:, 0:8], ln1b_h)
    ld(ln2w_s[:, :], ln2w_h)
    ld(lnraw[:, 8:16], ln2b_h)
    ld(rows_s[0:1, 0:D], qb_h)
    ld(rows_s[0:1, D:2 * D], kb_h)
    ld(rows_s[0:1, 2 * D:3 * D], vb_h)
    ld(rows_s[0:1, 3 * D:3 * D + H], decb_h)
    ld(wst_s[:, :], wst_h)
    ld(U_s[:, :], U_h)
    ld(negU_s[:, :], negU_h)
    ld(SL_s[:, :], SL_h)
    ld(ones128_s[:, :], ones128_h)
    ld(eye16_s[:, :], eye16_h)
    ld(eye1_s[:, :], eye1_h)
    ld(tiles(qw_s), qw_h.rearrange("(t p) n -> p t n", p=128))
    ld(tiles(kw_s), kw_h.rearrange("(t p) n -> p t n", p=128))
    ld(tiles(vw_s), vw_h.rearrange("(t p) n -> p t n", p=128))
    ld(tiles(decw_s), decw_h.rearrange("(t p) n -> p t n", p=128))
    ld(tiles(wstage), wq_h.rearrange("(t p) n -> p t n", p=128), tag="wq")
    NLD0 = p.val("ld")

    p.wait("vector", "ld", NLD0)
    cp("vector", ln1b16[:, :], lnraw[:, 0:8])
    cp("vector", ln2b16[:, :], lnraw[:, 8:16])
    for b in range(2):
        mset("vector", st2[b][:, EPS:EPS + 1], 1e-5)
    mset("vector", mrc[:, 0:1], MR)
    mset("vector", mrc[:, 1:2], -MR)
    mset("vector", ones1f[0:1, :], 1.0, inc="pv")
    PV1 = p.val("pv")

    p.wait("tensor", "pv", PV1)
    for w_s, off, width in ((qw_s, 0, D), (kw_s, D, D), (vw_s, 2 * D, D),
                            (decw_s, 3 * D, H)):
        for h2 in range(ceil_div(width, 512)):
            w0, wlen = h2 * 512, min(512, width - h2 * 512)
            acquire("D")
            for t in range(8):
                mm(pD[0:1, w0:w0 + wlen], ln1b16[:, t:t + 1],
                   tiles(w_s)[:, t, w0:w0 + wlen], start=(t == 0), stop=(t == 7),
                   inc=("Df" if t == 7 else None))
            nfill["D"] += 1
            p.wait("vector", "Df", p.val("Df"))
            tt("vector", rows_s[0:1, off + w0:off + w0 + wlen],
               pD[0:1, w0:w0 + wlen], rows_s[0:1, off + w0:off + w0 + wlen],
               ALU.add, inc="De")
            p.wait("tensor", "De", p.val("De"))

    for w_s, width in ((qw_s, D), (kw_s, D), (vw_s, D), (decw_s, H)):
        for t in range(8):
            wv = tiles(w_s)[:, t, :]
            ts("vector", xq_fm[0][:, 0:width], wv, ln1w_s[:, t:t + 1], None, ALU.mult)
            cp("vector", wv, xq_fm[0][:, 0:width])
    mset("vector", st2[0][0:1, 8:9], 0.0, inc="pv")
    PV2 = p.val("pv")

    p.wait("tensor", "pv", PV2)
    acquire("D")
    for t in range(8):
        tp(pD[:, t:t + 1], rows_s[0:1, t * 128:(t + 1) * 128], eye1_s[0:1, 0:1])
    for t in range(8):
        tp(pD[:, 8 + t:9 + t], rows_s[0:1, D + t * 128:D + (t + 1) * 128],
           eye1_s[0:1, 0:1], inc=("Df" if t == 7 else None))
    nfill["D"] += 1
    acquire("A")
    tp(pA[0:16, 0:1], rows_s[0:1, 3 * D:3 * D + 16], eye1_s[0:1, 0:1], inc="Af")
    nfill["A"] += 1
    p.wait("vector", "Df", p.val("Df"))
    cp("vector", qbe_c[:, :], pD[:, 0:8])
    cp("vector", kbe_c[:, :], pD[:, 8:16], inc="De")
    p.wait("vector", "Af", p.val("Af"))
    cp("vector", decbe_c[:, :], pA[0:16, 0:1], inc="Ae")

    for row_off, tbl in ((2 * D, vb_t),):
        acquire("B")
        for h2 in range(2):
            mm(pB[:, h2 * 512:(h2 + 1) * 512], ones1f[0:1, :],
               rows_s[0:1, row_off + h2 * 512:row_off + (h2 + 1) * 512],
               start=True, stop=True, inc=("Bf" if h2 == 1 else None))
        nfill["B"] += 1
        p.wait("vector", "Bf", p.val("Bf"))
        cp("vector", tbl[:, :], pB[:, :], inc="Be")
        p.wait("tensor", "Be", p.val("Be"))

    p.wait("sync", "ld", ld_marks["wq"])
    for ot in range(8):
        dmaT(tiles(wqt_s)[:, :, ot * 128:(ot + 1) * 128], tiles(wstage)[:, ot, :],
             inc="wqTd")
    WQT = p.val("wqTd")
    p.wait("vector", "wqTd", WQT)
    mset("vector", st2[1][0:1, 8:9], 0.0, inc="pv")
    PREP_W = p.val("pv")

    prep_sc.close()
    phik_tmg = SB1("phik_tmg", (128, 32 * 128), BF16)
    v_tm = [SB1(f"v_tm{i}", (128, 16 * KV), BF16) for i in range(2)]
    kef_e = SB1("kef_e", (128, 16 * 64), BF16)
    kef_f = SB1("kef_f", (128, 16 * 64), BF16)
    pq_r = [SB1(f"pq_r{i}", (128, 512), BF16) for i in range(2)]
    pq_e = [SB1(f"pq_e{i}", (128, 512), BF16) for i in range(2)]
    dabs = SB1("dabs", (128, 1024))
    dmat = SB1("dmat", (128, 1024), BF16)
    pt_sb = SB1("pt_sb", (128, 2048), BF16)
    fint_t = SB1("fint_t", (128, 8 * KV))
    outp_sb = [SB1("outp_sb0", (128, 16 * KV))] * 2
    S_f32 = [SB1(f"S_f32{i}", (128, 8 * KV)) for i in range(2)]
    S_b16 = SB1("S_b16", (128, 8 * KV), BF16)
    sc_tmp = SB1("sc_tmp", (128, 8 * KV))
    w_stg = [SB1("w_stg0", (128, 8 * KV))] * 2

    for sb_ in (S_f32[0], S_f32[1]):
        mset("vector", sb_[:, :], 0.0)
    mset("vector", S_b16[:, :], 0.0, inc="sinit")
    for i_ in range(2):
        mset("vector", v_tm[i_][:, :].rearrange("p (q v) -> p q v", v=KV)[:, :, 64:65], 1.0)
    mset("vector", st2[1][0:1, 9:10], 0.0, inc="pv")

    p.wait("tensor", "pv", PREP_W)
    p.wait("gpsimd", "pv", PREP_W)
    def front(c):
        bi = c % 2
        st = st2[bi]
        if c >= 2:
            p.wait("scalar", "v1", mval("red", c - 2))
            p.wait("scalar", "sq", mval("xn1tm", c - 2))
        XL = p.op("scalar", lambda e, o=xbuf[bi][:, :],
                  i=x_h[c * 128:(c + 1) * 128, :]: e.dma_start(o, i),
                  inc=f"xl{bi}", dma=True)
        mark("xl", c, XL)
        p.wait("vector", f"xl{bi}", XL)
        red("vector", st[:, SUM:SUM + 1], xbuf[bi][:, :], ALU.add, inc="v1")
        mark("red", c, p.val("v1"))
        p.wait("scalar", f"xl{bi}", XL)
        act(scr[:, :], xbuf[bi][:, :], AF.Square, accum=st[:, SQ:SQ + 1], inc="sq")
        mark("sqa", c, p.val("sq"))
        p.wait("vector", "sq", mval("sqa", c))
        ts("vector", st[:, M:M + 1], st[:, SUM:SUM + 1], INV_D, None, ALU.mult)
        tt("vector", st[:, VAR:VAR + 1], st[:, M:M + 1], st[:, M:M + 1], ALU.mult)
        stt("vector", st[:, V1:V1 + 1], st[:, SQ:SQ + 1], INV_D, st[:, VAR:VAR + 1],
            ALU.mult, ALU.subtract, inc="v1")
        mark("fst", c, p.val("v1"))

    def mid(c):
        g, cg = c // GSZ, c % GSZ
        bi = c % 2
        st = st2[bi]
        p.wait("scalar", "v1", mval("fst", c))
        act(st[:, R2:R2 + 1], st[:, V1:V1 + 1], AF.Ln, bias=st[:, EPS:EPS + 1])
        act(st[:, R_:R_ + 1], st[:, R2:R2 + 1], AF.Exp, scale=-0.5, inc="sq")
        p.wait("vector", "sq", p.val("sq"))
        tt("vector", st[:, VAR:VAR + 1], st[:, R_:R_ + 1], st[:, R_:R_ + 1], ALU.mult)
        tt("vector", st[:, R2:R2 + 1], st[:, VAR:VAR + 1], st[:, V1:V1 + 1],
           ALU.mult, inc="v1")
        p.wait("scalar", "v1", p.val("v1"))
        act(st[:, VAR:VAR + 1], st[:, R2:R2 + 1], AF.Ln, bias=st[:, EPS:EPS + 1])
        act(st[:, R2:R2 + 1], st[:, VAR:VAR + 1], AF.Exp, scale=-0.5, inc="sq")
        p.wait("vector", "sq", p.val("sq"))
        ts("vector", st[:, RR:RR + 1], st[:, M:M + 1], st[:, R2:R2 + 1], st[:, R_:R_ + 1],
           ALU.mult, ALU.mult)
        ts("vector", st[:, MB1:MB1 + 1], st[:, RR:RR + 1], -1.0, None, ALU.mult)
        ts("vector", st[:, RR:RR + 1], st[:, R_:R_ + 1], st[:, R2:R2 + 1], None,
           ALU.mult)
        ts("vector", st[:, MB:MB + 1], st[:, M:M + 1], st[:, R_:R_ + 1], -1.0,
           ALU.mult, ALU.mult, inc="v1")
        MBOK = p.val("v1")
        p.wait("scalar", "v1", MBOK)
        if c >= 2:
            p.wait("scalar", "v1", mval("amr", c - 2))
            p.wait("scalar", f"xnsp{bi}", mval("xnsp", c - 2))
        act(xnbuf[bi][:, :], xbuf[bi][:, :], AF.Identity, scale=st[:, R_:R_ + 1],
            bias=st[:, MB:MB + 1], inc="sq")
        mark("xnok", c, p.val("sq"))
        if c >= 2:
            p.wait("scalar", f"x1T{bi}", mval("xn1T", c - 2))
        act(xn1_tm[bi][:, :], xbuf[bi][:, :], AF.Identity, scale=st[:, RR:RR + 1],
            bias=st[:, MB1:MB1 + 1], inc="sq")
        mark("xn1tm", c, p.val("sq"))
        p.wait("gpsimd", "sq", p.val("sq"))
        mark("xnsp", c, dma(xn_sp[c * 128:(c + 1) * 128, :], xnbuf[bi][:, :],
                            inc=f"xnsp{bi}"))
        p.wait("sync", "sq", mval("xn1tm", c))
        if g >= 1 and cg == 0:
            p.wait("sync", "zg", mval("zg", g - 1))
            p.wait("sync", f"x1sp{(g - 1) % 2}", mval("xn1sp", g - 1))
        mark("xn1T", c,
             dmaT(S["xn1_g0"][:, :].rearrange("p (m r) -> p m r", r=512)
                  [:, :, cg * 128:(cg + 1) * 128], xn1_tm[bi][:, :],
                  inc=f"x1T{bi}"))

    def tail1(c):
        g, cg = c // GSZ, c % GSZ
        bi = c % 2
        gb = g % 2
        st = st2[bi]
        p.wait("vector", "sq", mval("xnok", c))
        red("vector", st[:, AM:AM + 1], xnbuf[bi][:, :], ALU.max, absval=True)
        ts("vector", st[:, XS:XS + 1], st[:, AM:AM + 1], 1e-6, 1.0 / 127.0, ALU.max,
           ALU.mult)
        rcp("vector", st[:, RS:RS + 1], st[:, XS:XS + 1], inc="v1")
        mark("amr", c, p.val("v1"))
        ts("vector", tbuf[bi][:, :], xnbuf[bi][:, :], st[:, RS:RS + 1], MR,
           ALU.mult, ALU.add)
        if c >= 2:
            p.wait("vector", f"xqT{bi}", mval("xqT", c - 2))
        ts("vector", xq_tm[bi][:, :], tbuf[bi][:, :], MR, None, ALU.subtract,
           inc="v1")
        mark("xqv", c, p.val("v1"))
        p.wait("sync", "v1", mval("xqv", c))
        if c >= 2:
            p.wait("sync", "Ae" if c % 2 == 0 else "Be", mval("qffn_ev", c - 2))
        mark("xqT", c, dmaT(S[f"xq_fm{bi}"][:, :].rearrange("p (m r) -> p m r", m=8),
                            xq_tm[bi][:, :], inc=f"xqT{bi}"))
        reg = "A" if c % 2 == 0 else "B"
        preg = PREG[reg]
        acquire(reg)
        p.wait("tensor", f"xqT{bi}", mval("xqT", c))
        p.wait("tensor", "wqTd", WQT)
        for t in range(8):
            for h2 in range(2):
                mm(preg[:, h2 * 512:(h2 + 1) * 512], xq_fm[bi][:, t * 128:(t + 1) * 128],
                   tiles(wqt_s)[:, t, h2 * 512:(h2 + 1) * 512],
                   start=(t == 0), stop=(t == 7),
                   inc=(reg + "f" if (t == 7 and h2 == 1) else None))
        nfill[reg] += 1
        mark("qfmm", c, p.val(reg + "f"))

        if cg == GSZ - 1:
            p.wait("tensor", f"x1T{c % 2}", mval("xn1T", c))
            p.wait("tensor", f"x1T{(c + 1) % 2}", mval("xn1T", c - 1))
            acquire("D")
            for t in range(8):
                mm(pD[0:16, 0:512], tiles(decw_s)[:, t, :], tiles(xn1_g[gb])[:, t, :],
                   start=(t == 0), stop=(t == 7), inc=("Df" if t == 7 else None))
            nfill["D"] += 1
            p.wait("vector", "Df", p.val("Df"))
            if g >= 1:
                p.wait("vector", "Ce", mval("ztr", g - 1))
            ts("vector", z_g[:, :], pD[0:16, 0:512], decbe_c[:, 0:1], None, ALU.add,
               inc="De")
            mark("zgv", g, p.val("De"))
            p.wait("tensor", "De", p.val("De"))
            acquire("C")
            for q in range(GSZ):
                tp(pC[:, q * 16:(q + 1) * 16], z_g[:, q * 128:(q + 1) * 128],
                   eye16_s[:, :], inc=("Cf" if q == GSZ - 1 else None))
            nfill["C"] += 1
            p.wait("vector", "Cf", p.val("Cf"))
            cp("vector", ztm[:, g * 64:(g + 1) * 64], pC[:, 0:64], inc="Ce")
            mark("ztr", g, p.val("Ce"))
            mset("vector", st2[0][0:1, 9:10], 0.0, inc="zg")
            mark("zg", g, p.val("zg"))
            p.wait("sync", f"x1T{c % 2}", mval("xn1T", c))
            p.wait("sync", f"x1T{(c + 1) % 2}", mval("xn1T", c - 1))
            mark("xn1sp", g, p.op("sync", lambda e, o=xn1_sp[g],
                                  i=xn1_g[gb][:, :]: e.dma_start(o, i),
                                  inc=f"x1sp{gb}", dma=True))

    def tail2(c):
        bi = c % 2
        st = st2[bi]
        reg = "A" if c % 2 == 0 else "B"
        preg = PREG[reg]
        p.wait("vector", reg + "f", mval("qfmm", c))
        if c >= 1:
            p.wait("vector", "qfsp", mval("qfsp", c - 1))
        stt("vector", qf_buf[bi][:, :], preg[:, :], st[:, XS:XS + 1], wst_s[:, :],
            ALU.mult, ALU.mult, inc=reg + "e")
        mark("qffn_ev", c, p.val(reg + "e"))
        p.wait("sync", reg + "e", p.val(reg + "e"))
        mark("qfsp", c, p.op("sync", lambda e,
                             o=qffn_sp[c * 128:(c + 1) * 128, :],
                             i=qf_buf[bi][:, :]: e.dma_start(o, i),
                             inc="qfsp", dma=True))

    for i_ in range(nch1a + 3):
        if i_ < nch1a:
            front(i_)
        if 2 <= i_ < nch1a + 2:
            tail1(i_ - 2)
        if 1 <= i_ < nch1a + 1:
            mid(i_ - 1)
        if i_ >= 3:
            tail2(i_ - 3)

    LAMTAB = {}
    if stage >= 2:
        p.wait("vector", "Ce", p.val("Ce"))
        mset("vector", st2[0][:, 10:11], 1.0)
        bitand("vector", lam_t1[:, :], ztm[:, :], 0x7FFFFFFF, inc="v2")
        p.wait("scalar", "v2", p.val("v2"))
        act(lam_t2[:, :], lam_t1[:, :], AF.Exp, scale=-1.0, inc="sc2")
        act(g_f32[:, :], lam_t2[:, :], AF.Ln, bias=st2[0][:, 10:11], inc="sc2")
        p.wait("vector", "sc2", p.val("sc2") - 1)
        ts("vector", lam_t1[:, :], ztm[:, :], 0.0, None, ALU.min)
        p.wait("vector", "sc2", p.val("sc2"))
        stt("vector", lam_t2[:, :], g_f32[:, :], -1.0, lam_t1[:, :], ALU.mult,
            ALU.add)
        ts("vector", g_f32[:, :], lam_t2[:, :], LOG1EM8, None, ALU.max)
        cp("vector", gh_tm[:, :], g_f32[:, :])
        tt("vector", lam_t2[:, :], g_f32[:, :], gh_tm[:, :], ALU.subtract)
        cp("vector", gl_tm[:, :], lam_t2[:, :], inc="v2")
        GOK = p.val("v2")
        p.wait("tensor", "v2", GOK)
        for r, lhs, outp, tab in (("A", U_s, pA, f_all), ("B", SL_s, pB, e_all)):
            acquire(r)
            mm(PREG[r][:, 0:512], lhs[:, :], gh_tm[:, :], start=True, stop=False)
            mm(PREG[r][:, 0:512], lhs[:, :], gl_tm[:, :], start=False, stop=True,
               inc=r + "f")
            nfill[r] += 1
            p.wait("scalar", r + "f", p.val(r + "f"))
            act(tab[:, :], PREG[r][:, 0:512], AF.Exp, inc=r + "e")
            LAMTAB[r] = p.val(r + "e")
        acquire("C")
        mm(pC[:, 0:512], ones128_s[:, :], gh_tm[:, :], start=True, stop=False)
        mm(pC[:, 0:512], ones128_s[:, :], gl_tm[:, :], start=False, stop=True,
           inc="Cf")
        nfill["C"] += 1
        p.wait("scalar", "Cf", p.val("Cf"))
        act(Lam_all[:, :], pC[:, 0:512], AF.Exp, inc="Ce")
        LAMTAB["C"] = p.val("Ce")


    mk2 = {}
    def mark2(name, c, sem, val):
        mk2[(name, c)] = (sem, val)
    def wmark(engine, name, c):
        if (name, c) in mk2:
            sem, val = mk2[(name, c)]
            p.wait(engine, sem, val)

    if stage >= 3:
        p.wait("vector", "Ae", LAMTAB["A"])
        p.wait("vector", "Be", LAMTAB["B"])
        p.wait("vector", "Ce", LAMTAB["C"])
        p.wait("gpsimd", "Ce", LAMTAB["C"])
        def _emit_diff_only(c, cg, bi):
            for w in range(2):
                regd = "A" if w == 0 else "B"
                pregd = PREG[regd]
                hh0 = w * 8
                acquire(regd)
                for hh in range(8):
                    blk_ = pregd[:, hh * 128:(hh + 1) * 128]
                    mm(blk_, negU_s[:, :], U_s[:, :], start=True, stop=True,
                       inc=(regd + "f" if hh == 7 else None))
                nfill[regd] += 1
                p.wait("vector", regd + "f", p.val(regd + "f"))
                cp("vector", dabs[:, :], pregd[:, :], inc=regd + "e")

        for g in range(ng1c):
            c0 = g * GSZ
            gp_ = g % 2
            if g >= 1:
                wmark("scalar", "lm", c0 - 1)
                p.wait("scalar", f"phq{(g - 1) % 2}", mval("phq", g - 1))
            else:
                p.wait("scalar", "De", mval("zgv", NG - 1))
                p.wait("scalar", f"x1sp{(NG - 1) % 2}", mval("xn1sp", NG - 1))
            XLG = p.op("scalar", lambda e, o=tiles(xn1_g[0]),
                       i=xn1_sp[g].rearrange("p (t n) -> p t n", t=8):
                       e.dma_start(o, i), inc=f"x1ld{gp_}", dma=True)
            p.wait("tensor", f"x1ld{gp_}", XLG)

            for which, w_s, bcol in (("q", qw_s, qbe_c), ("k", kw_s, kbe_c)):
                for dt in range(8):
                    reg = "A" if dt % 2 == 0 else "B"
                    preg = PREG[reg]
                    acquire(reg)
                    for t in range(8):
                        mm(preg[:, 0:512], tiles(w_s)[:, t, dt * 128:(dt + 1) * 128],
                           tiles(xn1_g[0])[:, t, :], start=(t == 0), stop=(t == 7),
                           inc=(reg + "f" if t == 7 else None))
                    nfill[reg] += 1
                    p.wait("scalar", reg + "f", p.val(reg + "f"))
                    act(pq_r[dt % 2][:, :], preg[:, 0:512], AF.Relu,
                        bias=bcol[:, dt:dt + 1])
                    act(pq_e[dt % 2][:, :], preg[:, 0:512], AF.Exp,
                        bias=bcol[:, dt:dt + 1], inc="sc3")
                    p.wait("vector", "sc3", p.val("sc3"))
                    if g >= 1 and dt == 0 and which == "q":
                        wmark("vector", "lm", c0 - 1)
                        p.wait("vector", f"phq{(g - 1) % 2}", mval("phq", g - 1))
                    dst = phiq_g[0] if which == "q" else phik_fm[0]
                    stt("vector", tiles(dst)[:, dt, :], pq_e[dt % 2][:, :], 1.0,
                        pq_r[dt % 2][:, :], ALU.min, ALU.add, inc=reg + "e")
                if which == "q" and sub3 != 27:
                    p.wait("sync", "Ae", p.val("Ae"))
                    p.wait("sync", "Be", p.val("Be"))
                    mark("phq", g, p.op("sync", lambda e,
                                        o=phiq_sp[g].rearrange("p (t n) -> p t n", t=8),
                                        i=tiles(phiq_g[0]): e.dma_start(o, i),
                                        inc=f"phq{gp_}", dma=True))
            PKT = 0
            if sub3 not in (26, 31, 32):
                p.wait("sync", "Ae", p.val("Ae"))
                p.wait("sync", "Be", p.val("Be"))
                if g >= 1:
                    wmark("sync", "kef", c0 - 1)
                PKT = dmaT(phik_tmg[:, :].rearrange("p (m r) -> p m r", m=32),
                           phik_fm[0][:, :], inc=f"pkT{gp_}")

            for cg in range(GSZ):
                if sub3 < 2:
                    continue
                SKIPVK = (sub3 == 28)
                VK31 = sub3 in (31, 32)
                c = c0 + cg
                bi = c % 2
                if SKIPVK:
                    _emit_diff_only(c, cg, bi)
                    continue
                acquire("C")
                for t in range(8):
                    for h2 in range(2):
                        mm(pC[:, h2 * 512:(h2 + 1) * 512],
                           tiles(xn1_g[0])[:, t, cg * 128:(cg + 1) * 128],
                           tiles(vw_s)[:, t, h2 * 512:(h2 + 1) * 512],
                           start=(t == 0), stop=(t == 7),
                           inc=("Cf" if (t == 7 and h2 == 1) else None))
                nfill["C"] += 1
                p.wait("vector", "Cf", p.val("Cf"))
                if c >= 2:
                    wmark("vector", "lm", c - 2)
                tt("vector",
                   v_tm[bi][:, :].rearrange("p (q v) -> p q v", v=KV)[:, :, 0:64],
                   pC[:, :].rearrange("p (q v) -> p q v", v=64),
                   vb_t[:, :].rearrange("p (q v) -> p q v", v=64), ALU.add, inc="Ce")
                VTM = p.val("Ce")

                if VK31 and sub3 == 31:
                    _emit_diff_only(c, cg, bi)
                    continue
                if sub3 not in (26, 31, 32):
                    p.wait("vector", f"pkT{gp_}", PKT)
                if c >= 1:
                    wmark("vector", "lm", c - 1)
                ktm4 = bass.AP(S["phik_tmg"], cg * 128,
                               [[32 * 128, 128], [512, 8], [64, 2], [1, 64]])
                e4 = e_all[:, c * 16:(c + 1) * 16].rearrange("p (a b) -> p a b", a=8) \
                    .unsqueeze(3).broadcast_to((128, 8, 2, 64))
                f4 = f_all[:, c * 16:(c + 1) * 16].rearrange("p (a b) -> p a b", a=8) \
                    .unsqueeze(3).broadcast_to((128, 8, 2, 64))
                ke4 = kef_e[:, :].rearrange("p (a b v) -> p a b v", a=8, b=2)
                kf4 = kef_f[:, :].rearrange("p (a b v) -> p a b v", a=8, b=2)
                tt("vector", ke4, ktm4, e4, ALU.mult)
                tt("vector", kf4, ktm4, f4, ALU.mult, inc="kefv")
                mark("kef", c, p.val("kefv"))
                mark2("kef", c, "kefv", p.val("kefv"))
                if VK31:
                    _emit_diff_only(c, cg, bi)
                    continue

                if sub3 < 3 and sub3 not in (22, 23, 24, 25, 26, 27, 29, 30, 31, 32):
                    continue
                for w in range(2):
                    regd = "A" if w == 0 else "B"
                    pregd = PREG[regd]
                    hh0 = w * 8
                    acquire(regd)
                    for pos, hh in enumerate(PERM8):
                        col = c * 16 + hh0 + hh
                        blk_ = pregd[:, pos * 128:(pos + 1) * 128]
                        if sub3 == 29:
                            mm(blk_, negU_s[:, :], U_s[:, :], start=True, stop=True,
                               inc=(regd + "f" if pos == 7 else None))
                            continue
                        if sub3 == 30:
                            mm(blk_, gh_tm[:, col:col + 1].broadcast_to((128, 128)),
                               U_s[:, :], start=True, stop=True,
                               inc=(regd + "f" if pos == 7 else None))
                            continue
                        if sub3 in (22, 26, 27):
                            mm(blk_, negU_s[:, :],
                               gh_tm[:, col:col + 1].broadcast_to((128, 128)),
                               start=True, stop=True,
                               inc=(regd + "f" if pos == 7 else None))
                            continue
                        if sub3 == 23:
                            for si, src in enumerate((gh_tm, gl_tm)):
                                mm(blk_, src[:, col:col + 1].broadcast_to((128, 128)),
                                   U_s[:, :], start=(si == 0), stop=(si == 1),
                                   inc=(regd + "f" if (pos == 7 and si == 1) else None))
                            continue
                        for si, src in enumerate((gh_tm, gl_tm)):
                            mm(blk_, negU_s[:, :], src[:, col:col + 1].broadcast_to((128, 128)),
                               start=(si == 0), stop=False)
                        for si, src in enumerate((gh_tm, gl_tm)):
                            last = (pos == 7 and si == 1)
                            mm(blk_, src[:, col:col + 1].broadcast_to((128, 128)), U_s[:, :],
                               start=False, stop=(si == 1),
                               inc=(regd + "f" if last else None))
                    nfill[regd] += 1
                    if sub3 in (22, 23, 24, 26, 27, 29, 30):
                        p.wait("vector", regd + "f", p.val(regd + "f"))
                        cp("vector", dabs[:, :], pregd[:, :], inc=regd + "e")
                        continue
                    p.wait("scalar", regd + "f", p.val(regd + "f"))
                    if 2 * c + w >= 1:
                        wmark("scalar", "ptv", 2 * c + w - 1)
                    act(dabs[:, :], pregd[:, :], AF.Abs, inc=regd + "e")
                    act(dmat[:, :], dabs[:, :], AF.Exp, scale=-1.0, inc="dm")
                    DM = p.val("dm")
                    if sub3 in (22, 23, 24, 25, 26, 27, 29, 30):
                        continue
                    acquire(regd)
                    for pos, hh in enumerate(PERM8):
                        h = hh0 + hh
                        dt, half = h // 2, (h % 2) * 64
                        mm(pregd[:, pos * 128:(pos + 1) * 128],
                           tiles(phik_fm[0])[half:half + 64, dt, cg * 128:(cg + 1) * 128],
                           tiles(phiq_g[0])[half:half + 64, dt, cg * 128:(cg + 1) * 128],
                           start=True, stop=True,
                           inc=(regd + "f" if pos == 7 else None))
                    nfill[regd] += 1
                    p.wait("vector", regd + "f", p.val(regd + "f"))
                    p.wait("vector", "dm", DM)
                    if c >= 1 and w == 0:
                        wmark("vector", "lm", c - 1)
                    if sub3 == 33:
                        cp("vector", pt_sb[:, w * 1024:(w + 1) * 1024], pregd[:, :],
                           inc=regd + "e")
                    else:
                        tt("vector", pt_sb[:, w * 1024:(w + 1) * 1024], pregd[:, :],
                           dmat[:, :], ALU.mult, inc=regd + "e")
                    mark2("ptv", 2 * c + w, regd + "e", p.val(regd + "e"))

                if sub3 < 4 or sub3 >= 22:
                    continue
                for w in range(2):
                    acquire("C")
                    wmark("tensor", "ptv", 2 * c + w)
                    for pos, hh in enumerate(PERM8):
                        h = w * 8 + hh
                        mm(pC[:, pos * 128:pos * 128 + KV],
                           pt_sb[:, (w * 8 + pos) * 128:(w * 8 + pos + 1) * 128],
                           v_tm[bi][:, h * KV:(h + 1) * KV], start=True, stop=True,
                           inc=("Cf" if pos == 7 else None))
                    nfill["C"] += 1
                    acquire("D")
                    if c >= 1 and w == 0:
                        wmark("tensor", "sb", c - 1)
                    for pos, hh in enumerate(PERM8):
                        h = w * 8 + hh
                        dt, half = h // 2, (h % 2) * 64
                        mm(pD[:, pos * 128:pos * 128 + KV],
                           tiles(phiq_g[0])[half:half + 64, dt, cg * 128:(cg + 1) * 128],
                           S_b16[half:half + 64, dt * KV:(dt + 1) * KV], start=True,
                           stop=True, inc=("Df" if pos == 7 else None))
                    nfill["D"] += 1
                    fcol0 = c * 16 + w * 8
                    p.wait("scalar", "Df", p.val("Df"))
                    p.wait("scalar", "Ce", p.val("Ce"))
                    for pos in range(8):
                        hh = PERM8[pos]
                        act(fint_t[:, pos * KV:(pos + 1) * KV],
                            pD[:, pos * 128:pos * 128 + KV], AF.Copy,
                            scale=f_all[:, fcol0 + hh:fcol0 + hh + 1],
                            inc=("De" if pos == 7 else None))
                    p.wait("vector", "Cf", p.val("Cf"))
                    p.wait("vector", "De", p.val("De"))
                    pC4 = pC[:, :].rearrange("p (b q r) -> p b q r", b=2, q=4)[:, :, :, 0:KV]
                    fi4 = fint_t[:, :].rearrange("p (b q v) -> p b q v", b=2, q=4)
                    if c >= 1 and w == 0:
                        p.wait("vector", f"osp{1 - bi}", mval("opsp", c - 1))
                    ov4 = bass.AP(S["outp_sb0"], w * 8 * KV,
                                  [[16 * KV, 128], [KV, 2], [2 * KV, 4], [1, KV]])
                    tt("vector", ov4, pC4, fi4, ALU.add, inc="Ce")

                if sub3 < 5 or sub3 >= 22:
                    continue
                for w in range(2):
                    acquire("A")
                    acquire("B")
                    if w == 0:
                        wmark("tensor", "kef", c)
                        p.wait("tensor", "Ce", VTM)
                    for mat, (kefm, regd) in enumerate(((kef_e, "A"), (kef_f, "B"))):
                        pregd = PREG[regd]
                        for half in range(2):
                            r0 = 64 * half
                            for j in range(4):
                                h = w * 8 + 2 * j + half
                                last = (half == 1 and j == 3)
                                p.op("tensor", lambda e,
                                     o=pregd[r0:r0 + 64,
                                             half * 512 + j * 128:half * 512 + j * 128 + KV],
                                     l=kefm[:, h * 64:(h + 1) * 64],
                                     r=v_tm[bi][:, h * KV:(h + 1) * KV], tp_=(0, r0):
                                     e.matmul(o, l, r, start=True, stop=True,
                                              tile_position=tp_),
                                     inc=(regd + "f" if last else None))
                    nfill["A"] += 1
                    nfill["B"] += 1
                    if w == 1:
                        mark2("lm", c, "Bf", p.val("Bf"))
                    cur, nxt = S_f32[c % 2], S_f32[(c + 1) % 2]
                    dt0 = 4 * w
                    if w == 0:
                        if c >= 1:
                            wmark("gpsimd", "sb", c - 1)
                        else:
                            p.wait("gpsimd", "sinit", p.val("sinit"))
                        hzfence("gpsimd")
                        for half in range(2):
                            rows = slice(64 * half, 64 * half + 64)
                            Lam8h = bass.AP(S["Lam_all"],
                                            64 * half * (16 * NCH) + c * 16 + half,
                                            [[16 * NCH, 64], [2, 8], [0, KV]])
                            tt("gpsimd", sc_tmp[rows, :]
                               .rearrange("p (q v) -> p q v", v=KV),
                               cur[rows, :].rearrange("p (q v) -> p q v", v=KV),
                               Lam8h, ALU.mult,
                               inc=("smul" if half == 1 else None))
                    p.wait("vector", "Af", p.val("Af"))
                    p.wait("vector", "Bf", p.val("Bf"))
                    p.wait("vector", "smul", p.val("smul"))
                    for half in range(2):
                        rows = slice(64 * half, 64 * half + 64)
                        tt("vector", nxt[rows, dt0 * KV:(dt0 + 4) * KV]
                           .rearrange("p (q v) -> p q v", v=KV),
                           pA[rows, half * 512:half * 512 + 512]
                           .rearrange("p (q r) -> p q r", q=4)[:, :, 0:KV],
                           sc_tmp[rows, w * 4 * KV:(w * 4 + 4) * KV]
                           .rearrange("p (q v) -> p q v", v=KV), ALU.add,
                           inc=("Ae" if half == 1 else None))
                        if c >= 1 and w == 0 and half == 0:
                            p.wait("vector", f"wsp{1 - bi}", mval("wsp", c - 1))
                        cp("vector",
                           w_stg[0][rows, dt0 * KV:(dt0 + 4) * KV]
                           .rearrange("p (q v) -> p q v", v=KV),
                           pB[rows, half * 512:half * 512 + 512]
                           .rearrange("p (q r) -> p q r", q=4)[:, :, 0:KV],
                           inc=("Be" if half == 1 else None))
                cp("vector", S_b16[:, :], nxt[:, :], inc="sb")
                mark2("sb", c, "sb", p.val("sb"))
                p.wait("sync", "Ce", p.val("Ce"))
                p.wait("sync", "Ae", p.val("Ae"))
                p.wait("sync", "Be", p.val("Be"))
                mark("opsp", c, p.op("sync", lambda e, o=op_sp[c],
                                     i=outp_sb[0][:, :]: e.dma_start(o, i),
                                     inc=f"osp{bi}", dma=True))
                mark("wsp", c, p.op("sync", lambda e, o=w_sp[c],
                                    i=w_stg[0][:, :]: e.dma_start(o, i),
                                    inc=f"wsp{bi}", dma=True))


    def engine_fence(waits):
        for e in Prog.ENGINES:
            for nm, v in waits:
                p.wait(e, nm, v)

    if stage >= 4:
        FIN1 = [(nm, p.val(nm)) for nm in
                ("Ae", "Be", "Ce", "De", "Af", "Bf", "Cf", "Df", "sb", "kefv",
                 "sc3", "dm", "v1", "v2", "sq", "qfsp",
                 "osp0", "osp1", "wsp0", "wsp1", "phq0", "phq1",
                 "x1ld0", "x1ld1", "pkT0", "pkT1", "xqT0", "xqT1", "x1T0", "x1T1",
                 "xnsp0", "xnsp1", "x1sp0", "x1sp1", "wqTd", "xl0", "xl1",
                 "smul", "sinit")]
        p1.close()
        p2a = ExitStack(); SB2 = mk_sb(p2a)
        ow_s = SB2("ow_s", (128, 8 * D), BF16)
        obt_s = SB2("obt_s", (128, D))
        phiq_2 = SB2("phiq_2", (128, 8 * 512), BF16)
        opld = [SB2(f"opld{i}", (128, 16 * KV)) for i in range(2)]
        wld = [SB2(f"wld{i}", (128, 8 * KV)) for i in range(2)]
        R_f32 = [SB2(f"R_f32{i}", (128, 8 * KV)) for i in range(2)]
        R_b16 = SB2("R_b16", (128, 8 * KV), BF16)
        rtmp = SB2("rtmp", (128, 8 * KV))
        bint_t = SB2("bint_t", (128, 16 * KV))
        xsc = SB2("xsc", (128, D))
        numf = SB2("numf", (128, 16 * KV))
        rden = SB2("rden", (128, 32))
        attn_tm = [SB2(f"attn_tm{i}", (128, D), BF16) for i in range(2)]
        attn_fm = [SB2(f"attn_fm{i}", (128, 8 * 128), BF16) for i in range(2)]
        xnld = [SB2(f"xnld{i}", (128, D)) for i in range(2)]
        xmid_b = [SB2(f"xmid_b{i}", (128, D)) for i in range(2)]
        engine_fence(FIN1)
        dma(tiles(ow_s), ow_h.rearrange("(t p) n -> p t n", p=128), inc="ld2")
        dma(obt_s[:, :], obt_h, inc="ld2")
        LD2 = p.val("ld2")
        obt16h = SB2("obt16h", (1, D), BF16)
        obt16l = SB2("obt16l", (1, D), BF16)
        mset("vector", R_f32[0][:, :], 0.0)
        mset("vector", R_f32[1][:, :], 0.0)
        mset("vector", R_b16[:, :], 0.0, inc="rb")
        mark2("rb", NCH, "rb", p.val("rb"))
        p.wait("vector", "ld2", LD2)
        cp("vector", obt16h[0:1, :], obt_s[0:1, :])
        tt("vector", xmid_b[0][0:1, :], obt_s[0:1, :], obt16h[0:1, :],
           ALU.subtract)
        cp("vector", obt16l[0:1, :], xmid_b[0][0:1, :], inc="ob16")
        p.wait("tensor", "ob16", p.val("ob16"))
        p.wait("tensor", "ld2", LD2)

        def oreg(c_):
            return "C" if c_ % 2 == 0 else "D"

        def emit_oproj(cx):
            bx = cx % 2
            r_ = oreg(cx)
            preg_o = PREG[r_]
            acquire(r_)
            p.wait("tensor", f"atT{bx}", mval("attT", cx))
            for t in range(8):
                for h2 in range(2):
                    mm(preg_o[:, h2 * 512:(h2 + 1) * 512],
                       attn_fm[bx][:, t * 128:(t + 1) * 128],
                       tiles(ow_s)[:, t, h2 * 512:(h2 + 1) * 512],
                       start=(t == 0), stop=False)
            for h2 in range(2):
                for oi, osrc in enumerate((obt16h, obt16l)):
                    mm(preg_o[:, h2 * 512:(h2 + 1) * 512],
                       ones128_s[0:1, 0:128], osrc[0:1, h2 * 512:(h2 + 1) * 512],
                       start=False, stop=(oi == 1),
                       inc=(r_ + "f" if (h2 == 1 and oi == 1) else None))
            nfill[r_] += 1
            mark2("om", cx, r_ + "f", p.val(r_ + "f"))
            mark2("om2", cx, r_ + "f", p.val(r_ + "f"))

        def emit_xmid(cx):
            bx = cx % 2
            r = oreg(cx)
            wmark("vector", "om2", cx)
            p.wait("vector", f"xnl{bx}", mval("xnl", cx))
            wmark("vector", "xms", cx + 2)
            tt("vector", xmid_b[bx][:, :], PREG[r][:, :], xnld[bx][:, :],
               ALU.add, inc=r + "e")
            mark2("xm", cx, r + "e", p.val(r + "e"))
            p.wait("gpsimd", r + "e", p.val(r + "e"))
            XMS = dma(xmid_sp[cx * 128:(cx + 1) * 128, :], xmid_b[bx][:, :],
                      inc=f"xms{bx}")
            mark2("xms", cx, f"xms{bx}", XMS)

        for c in range(NCH - 1, -1, -1):
            g, cg = c // GSZ, c % GSZ
            bi = c % 2
            if c + 2 <= NCH - 1:
                emit_xmid(c + 2)
            if cg == GSZ - 1:
                wmark("gpsimd", "bm", (g + 1) * GSZ)
                p.wait("gpsimd", f"phq{g % 2}", mval("phq", g))
                PH2 = dma(tiles(phiq_2), phiq_sp[g].rearrange("p (t n) -> p t n", t=8),
                          inc=f"p2q{g % 2}")
                mark("p2q", g, PH2)
            wmark("scalar", "nc", c + 2)
            wmark("scalar", "nc2", c + 2)
            wmark("scalar", "ncw", c + 2)
            p.wait("scalar", f"osp{bi}", mval("opsp", c))
            p.wait("scalar", f"wsp{bi}", mval("wsp", c))
            OPL = p.op("scalar", lambda e, o=opld[bi][:, :], i=op_sp[c]:
                       e.dma_start(o, i), inc=f"opl{bi}", dma=True)
            WL = p.op("scalar", lambda e, o=wld[bi][:, :], i=w_sp[c]:
                      e.dma_start(o, i), inc=f"wl{bi}", dma=True)
            wmark("sync", "xm", c + 2)
            p.wait("sync", f"xnsp{bi}", mval("xnsp", c))
            XNL = p.op("sync", lambda e, o=xnld[bi][:, :],
                       i=xn_sp[c * 128:(c + 1) * 128, :]: e.dma_start(o, i),
                       inc=f"xnl{bi}", dma=True)
            mark("xnl", c, XNL)

            p.wait("tensor", f"p2q{g % 2}", mval("p2q", g))
            wmark("tensor", "rb", c + 1)
            for w in range(2):
                regd = "A" if w == 0 else "B"
                pregd = PREG[regd]
                acquire(regd)
                for pos, hh in enumerate(PERM8):
                    h = w * 8 + hh
                    dt, half = h // 2, (h % 2) * 64
                    mm(pregd[:, pos * 128:pos * 128 + KV],
                       tiles(phiq_2)[half:half + 64, dt, cg * 128:(cg + 1) * 128],
                       R_b16[half:half + 64, dt * KV:(dt + 1) * KV], start=True,
                       stop=True, inc=(regd + "f" if pos == 7 else None))
                nfill[regd] += 1
                if w == 1:
                    mark2("bm", c, "Bf", p.val("Bf"))
            for w in range(2):
                regd = "A" if w == 0 else "B"
                pregd = PREG[regd]
                if w == 0:
                    wmark("scalar", "gn", c + 1)
                else:
                    wmark("scalar", "nc2", c + 1)
                p.wait("scalar", regd + "f", p.val(regd + "f"))
                for pos in range(8):
                    hh = PERM8[pos]
                    col = c * 16 + w * 8 + hh
                    act(bint_t[:, (w * 8 + hh) * KV:(w * 8 + hh + 1) * KV],
                        pregd[:, pos * 128:pos * 128 + KV], AF.Copy,
                        scale=e_all[:, col:col + 1],
                        inc=(regd + "e" if pos == 7 else None))
            cur, nxt = R_f32[c % 2], R_f32[(c + 1) % 2]
            wmark("gpsimd", "rb", c + 1)
            hzfence("gpsimd")
            for half in range(2):
                rows = slice(64 * half, 64 * half + 64)
                Lam8h = bass.AP(S["Lam_all"],
                                64 * half * (16 * NCH) + c * 16 + half,
                                [[16 * NCH, 64], [2, 8], [0, KV]])
                tt("gpsimd", rtmp[rows, :].rearrange("p (q v) -> p q v", v=KV),
                   cur[rows, :].rearrange("p (q v) -> p q v", v=KV),
                   Lam8h, ALU.mult, inc=("gmul" if half == 1 else None))
            wmark("gpsimd", "at", c + 1)
            p.wait("gpsimd", f"opl{bi}", OPL)
            p.wait("gpsimd", "Ae", p.val("Ae"))
            tt("gpsimd", numf[:, 0:8 * KV], opld[bi][:, 0:8 * KV],
               bint_t[:, 0:8 * KV], ALU.add, inc="gnum")
            mark2("nc", c, "gnum", p.val("gnum"))
            mark2("gn", c, "gnum", p.val("gnum"))
            p.wait("vector", f"wl{bi}", WL)
            p.wait("vector", "gmul", p.val("gmul"))
            for half in range(2):
                rows = slice(64 * half, 64 * half + 64)
                tt("vector", nxt[rows, :].rearrange("p (q v) -> p q v", v=KV),
                   wld[bi][rows, :].rearrange("p (q v) -> p q v", v=KV),
                   rtmp[rows, :].rearrange("p (q v) -> p q v", v=KV), ALU.add,
                   inc=("radd" if half == 1 else None))
            mark2("ncw", c, "radd", p.val("radd"))
            p.wait("vector", "Bf", p.val("Bf"))
            cp("vector", R_b16[:, :], nxt[:, :], inc="rb")
            mark2("rb", c, "rb", p.val("rb"))
            p.wait("vector", "Be", p.val("Be"))
            p.wait("vector", f"opl{bi}", OPL)
            tt("vector",
               numf[:, 8 * KV:16 * KV].rearrange("p (q v) -> p q v", v=KV),
               opld[bi][:, 8 * KV:16 * KV].rearrange("p (q v) -> p q v", v=KV),
               bint_t[:, 8 * KV:16 * KV].rearrange("p (q v) -> p q v", v=KV),
               ALU.add, inc="vnum")
            mark2("nc2", c, "vnum", p.val("vnum"))
            p.wait("vector", "gnum", p.val("gnum"))
            den = bass.AP(S["numf"], 64, [[16 * KV, 128], [KV, 16]])
            ts("vector", rden[:, 0:16], den, 1e-6, None, ALU.max)
            rcp("vector", rden[:, 16:32], rden[:, 0:16])
            if c <= NCH - 3:
                wmark("vector", "atT", c + 2)
            num3 = bass.AP(S["numf"], 0, [[16 * KV, 128], [KV, 16], [1, 64]])
            r3 = rden[:, 16:32].unsqueeze(2).broadcast_to((128, 16, 64))
            tt("vector", attn_tm[bi][:, :].rearrange("p (q v) -> p q v", v=64),
               num3, r3, ALU.mult, inc="atv")
            ATV = p.val("atv")
            mark2("at", c, "atv", ATV)
            p.wait("sync", "atv", ATV)
            if c <= NCH - 3:
                wmark("sync", "om", c + 2)
            ATT = dmaT(S[f"attn_fm{bi}"][:, :].rearrange("p (m r) -> p m r", m=8),
                       attn_tm[bi][:, :], inc=f"atT{bi}")
            mark2("atT", c, f"atT{bi}", ATT)
            mark("attT", c, ATT)
            if c + 1 <= NCH - 1:
                emit_oproj(c + 1)

        emit_oproj(0)
        emit_xmid(1)
        emit_xmid(0)

    if stage >= 5:
        FIN2 = [(nm, p.val(nm)) for nm in
                ("Ae", "Be", "Ce", "De", "Af", "Bf", "Cf", "Df", "rb", "atv",
                 "atT0", "atT1", "xms0", "xms1", "opl0", "opl1", "wl0", "wl1",
                 "xnl0", "xnl1", "p2q0", "p2q1", "ld2", "gnum", "vnum",
                 "gmul", "radd", "ob16")]
        if debug:
            dma(z_dbg, ztm[:, :], inc="dbg")
            dma(f_dbg, f_all[:, :], inc="dbg")
            dma(e_dbg, e_all[:, :], inc="dbg")
            dma(L_dbg, Lam_all[:, :], inc="dbg")
            dma(g_dbg, g_f32[:, :], inc="dbg")
            p.wait("gpsimd", "dbg", p.val("dbg"))
        p2a.close()
        lamS.close()
        p2b = ExitStack(); SB3 = mk_sb(p2b)
        f1w_s = SB3("f1w_s", (128, 8 * FFN), BF16)
        f2w_s = SB3("f2w_s", (128, 32 * D), BF16)
        row2 = SB3("row2", (1, 1024))
        f1be_c = SB3("f1be_c", (128, 32))
        h_fmb = SB3("h_fmb", (128, 32 * 512), BF16)
        x2fmb = SB3("x2fmb", (128, 4 * 1024), BF16)
        xmld = [SB3(f"xmld{i}", (128, D)) for i in range(2)]
        x2_tm = [SB3(f"x2_tm{i}", (128, D), BF16) for i in range(4)]
        obuf = [SB3(f"obuf{i}", (128, D)) for i in range(2)]
        xqb16h = SB3("xqb16h", (1, D), BF16)
        xqb16l = SB3("xqb16l", (1, D), BF16)
        engine_fence(FIN2)
        dma(tiles(f1w_s, 8), f1w_h.rearrange("(t p) n -> p t n", p=128), inc="ld3")
        dma(obuf[0][:, :], f2bt_h, inc="ld3")
        dma(obuf[1][:, :], qfbt_h, inc="ld3")
        LD3 = p.val("ld3")
        dma(f2w_s[:, :].rearrange("p (t n) -> p t n", t=32),
            f2w_h.rearrange("(t p) n -> p t n", p=128), inc="ldf2")
        LDF2 = p.val("ldf2")
        p.wait("vector", "ld3", LD3)
        tt("vector", row2[0:1, :], obuf[0][0:1, :], obuf[1][0:1, :], ALU.add)
        cp("vector", xqb16h[0:1, :], row2[0:1, :])
        tt("vector", obuf[0][0:1, :], row2[0:1, :], xqb16h[0:1, :],
           ALU.subtract)
        cp("vector", xqb16l[0:1, :], obuf[0][0:1, :], inc="v3")
        def prep2b():
            p.wait("tensor", "ld3", LD3)
            for q8 in range(4):
                q0 = q8 * 1024
                QL = dma(row2[0:1, :], f1b_h[0:1, q0:q0 + 1024], inc=f"r2l{q8 % 2}")
                p.wait("vector", f"r2l{q8 % 2}", QL)
                acquire("D")
                for h2 in range(2):
                    for t in range(8):
                        mm(pD[0:1, h2 * 512:(h2 + 1) * 512], ln2b16[:, t:t + 1],
                           tiles(f1w_s, 8)[:, t, q0 + h2 * 512:q0 + (h2 + 1) * 512],
                           start=(t == 0), stop=(t == 7),
                           inc=("Df" if (t == 7 and h2 == 1) else None))
                nfill["D"] += 1
                p.wait("vector", "Df", p.val("Df"))
                tt("vector", row2[0:1, :], pD[0:1, 0:1024], row2[0:1, :], ALU.add,
                   inc="De")
                p.wait("tensor", "De", p.val("De"))
                acquire("D")
                for t in range(8):
                    tp(pD[:, t:t + 1], row2[0:1, t * 128:(t + 1) * 128], eye1_s[0:1, 0:1],
                       inc=("Df" if t == 7 else None))
                nfill["D"] += 1
                p.wait("vector", "Df", p.val("Df"))
                cp("vector", f1be_c[:, q8 * 8:(q8 + 1) * 8], pD[:, 0:8], inc="De")
                p.wait("tensor", "De", p.val("De"))
                p.wait("gpsimd", "De", p.val("De"))
            for t in range(8):
                wv = tiles(f1w_s, 8)[:, t, :]
                hslice = h_fmb[:, (t % 2) * FFN:(t % 2 + 1) * FFN]
                if t >= 2:
                    p.wait("vector", "fsc", p.val("fsc"))
                ts("vector", hslice, wv, ln2w_s[:, t:t + 1], None, ALU.mult,
                   inc="fsv")
                p.wait("scalar", "fsv", p.val("fsv"))
                act(wv, hslice, AF.Copy, inc="fsc")
            p.wait("tensor", "fsc", p.val("fsc"))
            mset("vector", st2[0][0:1, 12:13], 0.0, inc="v3")


        def emit_tail(cc):
            bi = cc % 2
            r_ = "C" if bi == 0 else "D"
            wmark("scalar", "f2f", cc)
            wmark("scalar", "outw", cc - 2)
            act(obuf[bi][:, :], PREG[r_][:, :], AF.Copy, inc=r_ + "e")
            EV = p.val(r_ + "e")
            p.wait("gpsimd", r_ + "e", EV)
            for asrc in (xmid_sp[cc * 128:(cc + 1) * 128, :],
                         x_h[cc * 128:(cc + 1) * 128, :],
                         qffn_sp[cc * 128:(cc + 1) * 128, :]):
                AXW = p.op("gpsimd", lambda e, o=obuf[bi][:, :], i=asrc:
                           e.dma_start(o, i, accum_op=ALU.add),
                           inc=f"ax{bi}", dma=True)
                p.wait("gpsimd", f"ax{bi}", AXW)
            mark2("outw", cc, f"outw{bi}",
                  dma(out_h[cc * 128:(cc + 1) * 128, :], obuf[bi][:, :],
                      inc=f"outw{bi}"))

        def ln2blk(sI):
            for q in range(4):
                cc = 4 * sI + q
                bi = cc % 2
                st = st2[bi]
                wmark("gpsimd", "lnd", cc - 2)
                wmark("gpsimd", "xms", cc)
                XML = dma(xmld[bi][:, :], xmid_sp[cc * 128:(cc + 1) * 128, :],
                          inc=f"xml{bi}")
                p.wait("vector", f"xml{bi}", XML)
                red("vector", st[:, SUM:SUM + 1], xmld[bi][:, :], ALU.add,
                    inc="v3")
                p.wait("scalar", "v3", p.val("v3"))
                act(scr[:, :], xmld[bi][:, :], AF.Square,
                    accum=st[:, SQ:SQ + 1], inc="sq3")
                p.wait("vector", "sq3", p.val("sq3"))
                ts("vector", st[:, M:M + 1], st[:, SUM:SUM + 1], INV_D, None,
                   ALU.mult)
                tt("vector", st[:, VAR:VAR + 1], st[:, M:M + 1], st[:, M:M + 1],
                   ALU.mult)
                stt("vector", st[:, R_:R_ + 1], st[:, SQ:SQ + 1], INV_D,
                    st[:, VAR:VAR + 1], ALU.mult, ALU.subtract, inc="v3")
                p.wait("scalar", "v3", p.val("v3"))
                act(st[:, R2:R2 + 1], st[:, R_:R_ + 1], AF.Ln,
                    bias=st[:, EPS:EPS + 1])
                act(st[:, R_:R_ + 1], st[:, R2:R2 + 1], AF.Exp, scale=-0.5,
                    inc="sq3")
                p.wait("vector", "sq3", p.val("sq3"))
                wmark("vector", "x2T", cc - 4)
                ts("vector", x2_tm[cc % 4][:, :], xmld[bi][:, :], st[:, M:M + 1],
                   st[:, R_:R_ + 1], ALU.subtract, ALU.mult, inc="v3")
                mark2("lnd", cc, "v3", p.val("v3"))
                mark2("x2v", cc, "v3", p.val("v3"))

        ln2blk(0)
        prep2b()
        for sI in range(NCH // 4):
            if sI >= 1:
                ln2blk(sI)
            if sI >= 1:
                for q in range(4):
                    emit_tail(4 * (sI - 1) + q)
            for q in range(4):
                cc = 4 * sI + q
                wmark("sync", "x2v", cc)
                if sI >= 1 and q == 0:
                    wmark("sync", "f1m", sI - 1)
                X2T = dmaT(x2fmb[:, q * 1024:(q + 1) * 1024]
                           .rearrange("p (m r) -> p m r", m=8),
                           x2_tm[cc % 4][:, :], inc="xtb")
            for qq in range(4):
                mark2("x2T", 4 * sI + qq, "xtb", X2T)
            p.wait("tensor", "xtb", X2T)
            for fp in range(16):
                regd = "A" if fp % 2 == 0 else "B"
                pregd = PREG[regd]
                acquire(regd)
                for fi in range(2):
                    ft = 2 * fp + fi
                    for t in range(8):
                        mov = bass.AP(S["x2fmb"], t * 128,
                                      [[4096, 128], [1024, 4], [1, 128]])
                        mm(pregd[:, fi * 512:(fi + 1) * 512],
                           tiles(f1w_s, 8)[:, t, ft * 128:(ft + 1) * 128],
                           mov, start=(t == 0), stop=(t == 7),
                           inc=(regd + "f" if (fi == 1 and t == 7) else None))
                nfill[regd] += 1
                if fp >= 14:
                    mark2("f1m", sI, regd + "f", p.val(regd + "f"))
                p.wait("scalar", regd + "f", p.val(regd + "f"))
                for fi in range(2):
                    ft = 2 * fp + fi
                    act(h_fmb[:, ft * 512:(ft + 1) * 512],
                        pregd[:, fi * 512:(fi + 1) * 512], AF.Gelu,
                        bias=f1be_c[:, ft:ft + 1],
                        inc=(regd + "e" if fi == 1 else None))
            AEV, BEV = p.val("Ae"), p.val("Be")
            for q in range(4):
                cc = 4 * sI + q
                bi = cc % 2
                r_ = "C" if bi == 0 else "D"
                pr = PREG[r_]
                acquire(r_)
                if q == 0:
                    p.wait("tensor", "Ae", AEV)
                    p.wait("tensor", "Be", BEV)
                    if sI == 0:
                        p.wait("tensor", "ldf2", LDF2)
                for t in range(32):
                    for h2 in range(2):
                        mm(pr[:, h2 * 512:(h2 + 1) * 512],
                           h_fmb[:, t * 512 + q * 128:t * 512 + q * 128 + 128],
                           f2w_s[:, :].rearrange("p (t n) -> p t n", t=32)
                           [:, t, h2 * 512:(h2 + 1) * 512],
                           start=(t == 0), stop=False)
                for h2 in range(2):
                    for oi, osrc in enumerate((xqb16h, xqb16l)):
                        mm(pr[:, h2 * 512:(h2 + 1) * 512], ones128_s[0:1, 0:128],
                           osrc[0:1, h2 * 512:(h2 + 1) * 512], start=False,
                           stop=(oi == 1),
                           inc=(r_ + "f" if (h2 == 1 and oi == 1) else None))
                nfill[r_] += 1
                mark2("f2f", cc, r_ + "f", p.val(r_ + "f"))

        for q in range(4):
            emit_tail(NCH - 4 + q)

    if debug and stage < 5:
        p.wait("gpsimd", "Ce", p.val("Ce"))
        dma(z_dbg, ztm[:, :], inc="dbg")
        if stage >= 2:
            dma(f_dbg, f_all[:, :], inc="dbg")
            dma(e_dbg, e_all[:, :], inc="dbg")
            dma(L_dbg, Lam_all[:, :], inc="dbg")
            dma(g_dbg, g_f32[:, :], inc="dbg")
    if stage >= 5:
        p.wait("gpsimd", "outw0", p.val("outw0"))
        p.wait("gpsimd", "outw1", p.val("outw1"))
    else:
        for nm in ("xnsp0", "xnsp1", "qfsp", "x1sp0", "x1sp1", "dbg", "phq0", "phq1",
                   "osp0", "osp1", "wsp0", "wsp1"):
            p.wait("gpsimd", nm, p.val(nm))
        for nm in ("wqTd", "xqT0", "xqT1", "x1T0", "x1T1", "pkT0", "pkT1"):
            p.wait("sync", nm, p.val(nm))
        p.wait("gpsimd", "v1", p.val("v1"))
        dma(out_h[0:128, :], xbuf[0][:, :], inc="outw")
        p.wait("gpsimd", "outw", p.val("outw"))

    with nc.Block() as block:
        p.finalize(block)
    if stage >= 5:
        p2b.close()
    if stage >= 4:
        if stage < 5:
            p2a.close()
            lamS.close()
    else:
        p1.close()
        lamS.close()
    perm.close()
    return nc, ctx


class _closing:
    def __init__(self, st):
        self.st = st
    def __enter__(self):
        return self
    def __exit__(self, *a):
        self.st.close()



def _consts():
    s = np.arange(C)
    U = (s[:, None] <= s[None, :]).astype(np.float32)
    SL = (s[:, None] > s[None, :]).astype(np.float32)
    return {
        "Ut": U.astype(BF),
        "negUt": (-U).astype(BF),
        "SLt": SL.astype(BF),
        "ones128": np.ones((C, C), BF),
        "eye16": np.eye(16, dtype=np.float32),
        "eye1": np.array([[1.0]], np.float32),
    }


def make_inputs(full, core):
    i = {}
    i["x"] = full["L0_attn_x"][core]
    i["qw"] = full["L0_attn_q_w"]; i["kw"] = full["L0_attn_k_w"]
    i["vw"] = full["L0_attn_v_w"]; i["ow"] = full["L0_attn_o_w"]
    i["decw"] = full["L0_attn_dec_w"]
    i["f1w"] = full["L0_attn_f1_w"]; i["f2w"] = full["L0_attn_f2_w"]
    i["wq"] = full["L0_ffn_weight_q"]
    i["qb"] = full["L0_attn_q_b"][None, :]; i["kb"] = full["L0_attn_k_b"][None, :]
    i["vb"] = full["L0_attn_v_b"][None, :]
    i["decb"] = full["L0_attn_dec_b"][None, :]
    i["f1b"] = full["L0_attn_f1_b"][None, :]
    i["ln1w"] = full["L0_attn_ln1_w"].reshape(8, 128).T
    i["ln1b"] = full["L0_attn_ln1_b"].reshape(8, 128).T
    i["ln2w"] = full["L0_attn_ln2_w"].reshape(8, 128).T
    i["ln2wr"] = full["L0_attn_ln2_w"][None, :]
    i["ln2b"] = full["L0_attn_ln2_b"].reshape(8, 128).T
    i["obt"] = np.tile(full["L0_attn_o_b"][None, :], (128, 1))
    i["f2bt"] = np.tile(full["L0_attn_f2_b"][None, :], (128, 1))
    i["qfbt"] = np.tile(full["L0_ffn_bias"][None, :], (128, 1))
    i["wst"] = np.tile(full["L0_ffn_weight_scale"][None, :], (128, 1))
    i.update(_consts())
    return {k: np.ascontiguousarray(v) for k, v in i.items()}


def kernel(**inputs):
    nc, ctx = build_program(debug=False)
    in_maps = [make_inputs(inputs, core) for core in range(8)]
    res = run_bass_kernel_spmd(nc, in_maps, core_ids=list(range(8)))
    ctx.close()
    return np.stack([np.asarray(r["out"], np.float32) for r in res.results], axis=0)


# version 18 (final)
# speedup vs baseline: 1.0431x; 1.0431x over previous
import numpy as np
import ml_dtypes
import concourse.bass as bass
import concourse.mybir as mybir
from concourse.bass_utils import run_bass_kernel_spmd

F32 = mybir.dt.float32
BF16 = mybir.dt.bfloat16
I32 = mybir.dt.int32
I8 = mybir.dt.int8
AF = mybir.ActivationFunctionType
ALU = mybir.AluOpType
AX = mybir.AxisListType
BF = ml_dtypes.bfloat16

T, D, H, K = 4096, 1024, 16, 64
C, NCH, GSZ = 128, 32, 4
NG = NCH // GSZ
FFN = 4096
KV = K + 1
MR = 12582912.0
LOG1EM8 = float(np.log(np.float32(1e-8)))
INV_D = 1.0 / D


def ceil_div(a, b):
    return (a + b - 1) // b


class Prog:
    ENGINES = ("tensor", "vector", "scalar", "gpsimd", "sync")

    def __init__(self, nc, ctx):
        self.nc = nc
        self.prog = {e: [] for e in self.ENGINES}
        self.sems = {}
        self.cnt = {}
        self.ctx = ctx

    def _sem(self, name):
        if name not in self.sems:
            self.sems[name] = self.ctx.enter_context(self.nc.semaphore(name))
            self.cnt[name] = 0
        return self.sems[name]

    def wait(self, engine, name, val=None):
        s = self._sem(name)
        v = self.cnt[name] if val is None else val
        if v > 0:
            self.prog[engine].append(lambda eng, s=s, v=v: eng.wait_ge(s, v))

    def op(self, engine, fn, inc=None, dma=False):
        if inc is None:
            self.prog[engine].append(fn)
            return None
        s = self._sem(inc)
        a = 16 if dma else 1
        self.cnt[inc] += a
        self.prog[engine].append(lambda eng, s=s, a=a, fn=fn: fn(eng).then_inc(s, a))
        return self.cnt[inc]

    def val(self, name):
        self._sem(name)
        return self.cnt[name]

    def barrier(self, name):
        s = self._sem(name)
        base = self.cnt[name]
        self.cnt[name] += len(self.ENGINES)
        for e in self.ENGINES:
            if e == "tensor":
                self.prog[e].append(lambda eng, s=s: eng.nop().then_inc(s, 1))
            elif e in ("vector", "gpsimd"):
                self.prog[e].append(lambda eng, s=s: eng.engine_nop().then_inc(s, 1)
                                    if hasattr(eng, "engine_nop") else eng.nop().then_inc(s, 1))
            else:
                self.prog[e].append(lambda eng, s=s: eng.nop().then_inc(s, 1))
        for e in self.ENGINES:
            self.prog[e].append(lambda eng, s=s, v=base + len(self.ENGINES):
                                eng.wait_ge(s, v))

    def finalize(self, block):
        for engine in self.ENGINES:
            ops = list(self.prog[engine])
            def body(eng, ops=ops):
                for f in ops:
                    f(eng)
            getattr(block, engine)(body)


def build_program(debug=False, stage=99, nch1a=NCH, sub3=9, ng1c=NG):
    nc = bass.Bass("TRN2", target_bir_lowering=False)
    from contextlib import ExitStack
    ctx = ExitStack()

    def IN(name, shape, dt=F32):
        return nc.dram_tensor(name, list(shape), dt, kind="ExternalInput").ap()

    x_h = IN("x", (T, D))
    qw_h, kw_h, vw_h, ow_h = (IN(n, (D, D)) for n in ("qw", "kw", "vw", "ow"))
    decw_h = IN("decw", (D, H))
    f1w_h = IN("f1w", (D, FFN))
    f2w_h = IN("f2w", (FFN, D))
    wq_h = IN("wq", (D, D), I8)
    qb_h, kb_h, vb_h = IN("qb", (1, D)), IN("kb", (1, D)), IN("vb", (1, D))
    decb_h = IN("decb", (1, H))
    f1b_h = IN("f1b", (1, FFN))
    ln1w_h, ln1b_h = IN("ln1w", (128, 8)), IN("ln1b", (128, 8))
    ln2wr_h = IN("ln2wr", (1, D))
    ln2w_h, ln2b_h = IN("ln2w", (128, 8)), IN("ln2b", (128, 8))
    obt_h = IN("obt", (128, D))
    f2bt_h = IN("f2bt", (128, D))
    qfbt_h = IN("qfbt", (128, D))
    wst_h = IN("wst", (128, D))
    U_h = IN("Ut", (C, C), BF16)
    negU_h = IN("negUt", (C, C), BF16)
    SL_h = IN("SLt", (C, C), BF16)
    ones128_h = IN("ones128", (C, C), BF16)
    eye16_h = IN("eye16", (16, 16))
    eye1_h = IN("eye1", (1, 1))
    out_h = nc.dram_tensor("out", [T, D], F32, kind="ExternalOutput").ap()

    def SCR(name, shape, dt=F32):
        kind = "ExternalOutput" if debug else "Internal"
        return nc.dram_tensor(name, list(shape), dt, kind=kind).ap()

    xn_sp = SCR("xn_sp", (T, D))
    qffn_sp = SCR("qffn_sp", (T, D))
    xn1_sp = SCR("xn1_sp", (NG, 128, 8 * 512), BF16)
    phiq_sp = SCR("phiq_sp", (NG, 128, 8 * 512), BF16)
    w_sp = SCR("w_sp", (NCH, 128, 8 * KV))
    op_sp = SCR("op_sp", (NCH, 128, 16 * KV))
    xmid_sp = SCR("xmid_sp", (T, D))
    if debug:
        z_dbg = SCR("z_dbg", (128, 512))
        f_dbg = SCR("f_dbg", (128, 16 * NCH))
        e_dbg = SCR("e_dbg", (128, 16 * NCH))
        L_dbg = SCR("L_dbg", (128, 16 * NCH))
        g_dbg = SCR("g_dbg", (128, 16 * NCH))

    S = {}

    def mk_sb(stack):
        def SB(name, shape, dt=F32):
            S[name] = stack.enter_context(nc.sbuf_tensor(name, list(shape), dt))
            return S[name]
        return SB

    perm = ExitStack(); SBp = mk_sb(perm)
    lamS = ExitStack()
    ln1w_s = SBp("ln1w_s", (128, 8))
    ln2w_s = SBp("ln2w_s", (128, 8))
    ln1b16 = SBp("ln1b16", (128, 8), BF16)
    ln2b16 = SBp("ln2b16", (128, 8), BF16)
    lnraw = SBp("lnraw", (128, 16))
    decbe_c = SBp("decbe_c", (16, 1))
    qbe_c = SBp("qbe_c", (128, 8))
    kbe_c = SBp("kbe_c", (128, 8))
    U_s = SBp("U_s", (C, C), BF16)
    negU_s = SBp("negU_s", (C, C), BF16)
    SL_s = SBp("SL_s", (C, C), BF16)
    ones128_s = SBp("ones128_s", (C, C), BF16)
    eye16_s = SBp("eye16_s", (16, 16))
    eye1_s = SBp("eye1_s", (1, 1))
    ones1f = SBp("ones1f", (1, 128))
    st2 = [SBp(f"st{i}", (128, 16)) for i in range(2)]
    scr = SBp("scr", (128, D))
    dumV = SBp("dumV", (1, 8))
    dumS = SBp("dumS", (1, 8))
    dumG = SBp("dumG", (1, 8))
    SBl = mk_sb(lamS)
    z_g = SBl("z_g", (16, 512))
    ztm = SBl("ztm", (128, 16 * NCH))
    g_f32 = SBl("g_f32", (128, 16 * NCH))
    gh_tm = SBl("gh_tm", (128, 16 * NCH), BF16)
    gl_tm = SBl("gl_tm", (128, 16 * NCH), BF16)
    f_all = SBl("f_all", (128, 16 * NCH))
    e_all = SBl("e_all", (128, 16 * NCH))
    Lam_all = SBl("Lam_all", (128, 16 * NCH))
    vb_t = SBl("vb_t", (128, D))

    pA = perm.enter_context(nc.psum_tensor("pA", [128, 1024], F32))
    pB = perm.enter_context(nc.psum_tensor("pB", [128, 1024], F32))
    pC = perm.enter_context(nc.psum_tensor("pC", [128, 1024], F32))
    pD = perm.enter_context(nc.psum_tensor("pD", [128, 1024], F32))
    PREG = {"A": pA, "B": pB, "C": pC, "D": pD}

    p = Prog(nc, ctx)
    nfill = {"A": 0, "B": 0, "C": 0, "D": 0}

    prev_rec = {}

    def _ivals(aps):
        out = []
        for a in aps:
            if a is None or isinstance(a, (int, float)):
                continue
            try:
                name = a.tensor.name
            except Exception:
                continue
            try:
                W = a.tensor.shape[-1]
                dims = a.ap
                off = a.offset
                ok = dims[0][0] == W or dims[0][1] == 1
                for s, cnt in dims[1:]:
                    if cnt > 1 and abs(s) >= W:
                        ok = False
                if not ok or W <= 0:
                    out.append((name, 0, 1 << 40, 0, 1 << 40))
                    continue
                p0, c0 = off // W, off % W
                clo = c0 + sum(min(0, (cnt - 1) * s) for s, cnt in dims[1:])
                chi = c0 + sum(max(0, (cnt - 1) * s) for s, cnt in dims[1:])
                out.append((name, p0, p0 + dims[0][1] - 1, clo, chi))
            except Exception:
                out.append((name, 0, 1 << 40, 0, 1 << 40))
        return out

    def _clash(xs, ys):
        for nx, pl, ph, cl, ch in xs:
            for ny, ql, qh, dl, dh in ys:
                if nx == ny and pl <= qh and ql <= ph and cl <= dh and dl <= ch:
                    return True
        return False

    def hz(engine, outs, ins):
        o, i = _ivals(outs), _ivals(ins)
        wa, ra = prev_rec.setdefault(engine, ([], []))
        if _clash(wa, o + i) or _clash(ra, o):
            p.prog[engine].append(lambda e: e.drain())
            wa, ra = [], []
        prev_rec[engine] = (wa + o, ra + i)

    def hzfence(engine):
        prev_rec[engine] = ([], [])

    def acquire(r):
        p.wait("tensor", r + "e", nfill[r])

    def mm(out, lhsT, rhs, start, stop, inc=None):
        return p.op("tensor", lambda e, o=out, l=lhsT, r=rhs, s=start, t=stop:
                    e.matmul(o, l, r, start=s, stop=t), inc=inc)

    def tp(out, in_, ident, inc=None):
        return p.op("tensor", lambda e, o=out, i=in_, d=ident: e.transpose(o, i, d),
                    inc=inc)

    def ts(engine, out, in0, s1, s2, op0, op1=None, inc=None):
        hz(engine, [out], [in0, s1, s2])
        if op1 is None:
            return p.op(engine, lambda e, o=out, i=in0, a=s1: e.tensor_scalar(
                o, i, a, None, op0), inc=inc)
        return p.op(engine, lambda e, o=out, i=in0, a=s1, b=s2: e.tensor_scalar(
            o, i, a, b, op0, op1), inc=inc)

    def tt(engine, out, a, b, alu, inc=None):
        hz(engine, [out], [a, b])
        return p.op(engine, lambda e, o=out, x=a, y=b: e.tensor_tensor(o, x, y, alu),
                    inc=inc)

    def stt(engine, out, in0, scalar, in1, op0, op1, inc=None):
        hz(engine, [out], [in0, scalar, in1])
        return p.op(engine, lambda e, o=out, i=in0, s=scalar, j=in1:
                    e.scalar_tensor_tensor(o, i, s, j, op0, op1), inc=inc)

    def act(out, in_, func, bias=None, scale=None, accum=None, inc=None):
        hz("scalar", [out, accum], [in_, bias])
        kw = {}
        if bias is not None:
            kw["bias"] = bias
        if scale is not None:
            kw["scale"] = scale
        if accum is not None:
            kw["accum_out"] = accum
        return p.op("scalar", lambda e, o=out, i=in_, f=func, kw=kw:
                    e.activation(o, i, f, **kw), inc=inc)

    def cp(engine, out, in_, inc=None):
        hz(engine, [out], [in_])
        return p.op(engine, lambda e, o=out, i=in_: e.tensor_copy(o, i), inc=inc)

    def red(engine, out, in_, alu, absval=False, inc=None):
        hz(engine, [out], [in_])
        return p.op(engine, lambda e, o=out, i=in_: e.tensor_reduce(
            o, i, AX.X, alu, apply_absolute_value=absval), inc=inc)

    def mset(engine, ap, v, inc=None):
        hz(engine, [ap], [])
        return p.op(engine, lambda e, a=ap, vv=v: e.memset(a, vv), inc=inc)

    def rcp(engine, out, in_, inc=None):
        hz(engine, [out], [in_])
        return p.op(engine, lambda e, o=out, i=in_: e.reciprocal(o, i), inc=inc)

    def bitand(engine, out, in_, mask, inc=None):
        hz(engine, [out], [in_])
        return p.op(engine, lambda e, o=out, i=in_, m=mask: e.tensor_scalar(
            o.bitcast(I32), i.bitcast(I32), m, None, ALU.bitwise_and), inc=inc)

    def dma(dst, src, inc="gdma"):
        return p.op("gpsimd", lambda e, d=dst, s=src: e.dma_start(d, s), inc=inc,
                    dma=True)

    def dmaT(dst, src, inc="tdma"):
        return p.op("sync", lambda e, d=dst, s=src: e.dma_start_transpose(d, s),
                    inc=inc, dma=True)

    def tiles(h, n=8):
        return h[:, :].rearrange("p (t n) -> p t n", t=n)

    SUM, SQ, M, R_, AM, XS, RS, VAR, R2, EPS = 0, 1, 2, 3, 4, 5, 6, 7, 8, 15
    V1, MB1, RR, MB = 11, 12, 13, 14
    PERM8 = [0, 2, 4, 6, 1, 3, 5, 7]
    mk = {}
    def mark(name, c, v):
        mk[(name, c)] = v
    def mval(name, c):
        return mk.get((name, c), 0)

    p1 = ExitStack(); SB1 = mk_sb(p1)
    qw_s = SB1("qw_s", (128, 8 * D), BF16)
    kw_s = SB1("kw_s", (128, 8 * D), BF16)
    vw_s = SB1("vw_s", (128, 8 * D), BF16)
    decw_s = SB1("decw_s", (128, 8 * H), BF16)
    wqt_s = SB1("wqt_s", (128, 8 * D), BF16)
    wst_s = SB1("wst_s", (128, D))
    xbuf = [SB1(f"xbuf{i}", (128, D)) for i in range(2)]
    xnbuf = [SB1(f"xnbuf{i}", (128, D)) for i in range(2)]
    tbuf = [SB1(f"tbuf{i}", (128, D)) for i in range(2)]
    mrc = SB1("mrc", (128, 2))
    xq_tm = [SB1(f"xq_tm{i}", (128, D), BF16) for i in range(2)]
    xq_fm = [SB1(f"xq_fm{i}", (128, 8 * 128), BF16) for i in range(2)]
    xn1_tm = [SB1(f"xn1_tm{i}", (128, D), BF16) for i in range(2)]
    xn1_g = [SB1("xn1_g0", (128, 8 * 512), BF16)] * 2
    qf_buf = [SB1("qf_buf0", (128, D))] * 2
    lam_t1 = SB1("lam_t1", (128, 16 * NCH))
    lam_t2 = SB1("lam_t2", (128, 16 * NCH))
    phiq_g = [SB1("phiq_g0", (128, 8 * 512), BF16)] * 2
    phik_fm = [SB1("phik_fm0", (128, 8 * 512), BF16)] * 2

    prep_sc = ExitStack(); SBq = mk_sb(prep_sc)
    rows_s = SBq("rows_s", (1, 3 * D + 128))
    wstage = SBq("wstage", (128, 8 * D), BF16)

    ld_marks = {}
    def ld(dst, src, tag=None):
        v = dma(dst, src, inc="ld")
        if tag:
            ld_marks[tag] = v

    ld(ln1w_s[:, :], ln1w_h)
    ld(lnraw[:, 0:8], ln1b_h)
    ld(ln2w_s[:, :], ln2w_h)
    ld(lnraw[:, 8:16], ln2b_h)
    ld(rows_s[0:1, 0:D], qb_h)
    ld(rows_s[0:1, D:2 * D], kb_h)
    ld(rows_s[0:1, 2 * D:3 * D], vb_h)
    ld(rows_s[0:1, 3 * D:3 * D + H], decb_h)
    ld(wst_s[:, :], wst_h)
    ld(U_s[:, :], U_h)
    ld(negU_s[:, :], negU_h)
    ld(SL_s[:, :], SL_h)
    ld(ones128_s[:, :], ones128_h)
    ld(eye16_s[:, :], eye16_h)
    ld(eye1_s[:, :], eye1_h)
    ld(tiles(qw_s), qw_h.rearrange("(t p) n -> p t n", p=128))
    ld(tiles(kw_s), kw_h.rearrange("(t p) n -> p t n", p=128))
    ld(tiles(vw_s), vw_h.rearrange("(t p) n -> p t n", p=128))
    ld(tiles(decw_s), decw_h.rearrange("(t p) n -> p t n", p=128))
    ld(tiles(wstage), wq_h.rearrange("(t p) n -> p t n", p=128), tag="wq")
    NLD0 = p.val("ld")

    p.wait("vector", "ld", NLD0)
    cp("vector", ln1b16[:, :], lnraw[:, 0:8])
    cp("vector", ln2b16[:, :], lnraw[:, 8:16])
    for b in range(2):
        mset("vector", st2[b][:, EPS:EPS + 1], 1e-5)
    mset("vector", mrc[:, 0:1], MR)
    mset("vector", mrc[:, 1:2], -MR)
    mset("vector", ones1f[0:1, :], 1.0, inc="pv")
    PV1 = p.val("pv")

    p.wait("tensor", "pv", PV1)
    for w_s, off, width in ((qw_s, 0, D), (kw_s, D, D), (vw_s, 2 * D, D),
                            (decw_s, 3 * D, H)):
        for h2 in range(ceil_div(width, 512)):
            w0, wlen = h2 * 512, min(512, width - h2 * 512)
            acquire("D")
            for t in range(8):
                mm(pD[0:1, w0:w0 + wlen], ln1b16[:, t:t + 1],
                   tiles(w_s)[:, t, w0:w0 + wlen], start=(t == 0), stop=(t == 7),
                   inc=("Df" if t == 7 else None))
            nfill["D"] += 1
            p.wait("vector", "Df", p.val("Df"))
            tt("vector", rows_s[0:1, off + w0:off + w0 + wlen],
               pD[0:1, w0:w0 + wlen], rows_s[0:1, off + w0:off + w0 + wlen],
               ALU.add, inc="De")
            p.wait("tensor", "De", p.val("De"))

    for w_s, width in ((qw_s, D), (kw_s, D), (vw_s, D), (decw_s, H)):
        for t in range(8):
            wv = tiles(w_s)[:, t, :]
            ts("vector", xq_fm[0][:, 0:width], wv, ln1w_s[:, t:t + 1], None, ALU.mult)
            cp("vector", wv, xq_fm[0][:, 0:width])
    mset("vector", st2[0][0:1, 8:9], 0.0, inc="pv")
    PV2 = p.val("pv")

    p.wait("tensor", "pv", PV2)
    acquire("D")
    for t in range(8):
        tp(pD[:, t:t + 1], rows_s[0:1, t * 128:(t + 1) * 128], eye1_s[0:1, 0:1])
    for t in range(8):
        tp(pD[:, 8 + t:9 + t], rows_s[0:1, D + t * 128:D + (t + 1) * 128],
           eye1_s[0:1, 0:1], inc=("Df" if t == 7 else None))
    nfill["D"] += 1
    acquire("A")
    tp(pA[0:16, 0:1], rows_s[0:1, 3 * D:3 * D + 16], eye1_s[0:1, 0:1], inc="Af")
    nfill["A"] += 1
    p.wait("vector", "Df", p.val("Df"))
    cp("vector", qbe_c[:, :], pD[:, 0:8])
    cp("vector", kbe_c[:, :], pD[:, 8:16], inc="De")
    p.wait("vector", "Af", p.val("Af"))
    cp("vector", decbe_c[:, :], pA[0:16, 0:1], inc="Ae")

    for row_off, tbl in ((2 * D, vb_t),):
        acquire("B")
        for h2 in range(2):
            mm(pB[:, h2 * 512:(h2 + 1) * 512], ones1f[0:1, :],
               rows_s[0:1, row_off + h2 * 512:row_off + (h2 + 1) * 512],
               start=True, stop=True, inc=("Bf" if h2 == 1 else None))
        nfill["B"] += 1
        p.wait("vector", "Bf", p.val("Bf"))
        cp("vector", tbl[:, :], pB[:, :], inc="Be")
        p.wait("tensor", "Be", p.val("Be"))

    p.wait("sync", "ld", ld_marks["wq"])
    for ot in range(8):
        dmaT(tiles(wqt_s)[:, :, ot * 128:(ot + 1) * 128], tiles(wstage)[:, ot, :],
             inc="wqTd")
    WQT = p.val("wqTd")
    p.wait("vector", "wqTd", WQT)
    mset("vector", st2[1][0:1, 8:9], 0.0, inc="pv")
    PREP_W = p.val("pv")

    prep_sc.close()
    phik_tmg = SB1("phik_tmg", (128, 32 * 128), BF16)
    v_tm = [SB1(f"v_tm{i}", (128, 16 * KV), BF16) for i in range(2)]
    kef_e = SB1("kef_e", (128, 16 * 64), BF16)
    kef_f = SB1("kef_f", (128, 16 * 64), BF16)
    pq_r = [SB1(f"pq_r{i}", (128, 512), BF16) for i in range(2)]
    pq_e = [SB1(f"pq_e{i}", (128, 512), BF16) for i in range(2)]
    dabs = SB1("dabs", (128, 1024))
    dmat = SB1("dmat", (128, 1024), BF16)
    pt_sb = SB1("pt_sb", (128, 2048), BF16)
    fint_t = SB1("fint_t", (128, 8 * KV))
    outp_sb = [SB1("outp_sb0", (128, 16 * KV))] * 2
    S_f32 = [SB1(f"S_f32{i}", (128, 8 * KV)) for i in range(2)]
    S_b16 = SB1("S_b16", (128, 8 * KV), BF16)
    sc_tmp = SB1("sc_tmp", (128, 8 * KV))
    w_stg = [SB1("w_stg0", (128, 8 * KV))] * 2

    for sb_ in (S_f32[0], S_f32[1]):
        mset("vector", sb_[:, :], 0.0)
    mset("vector", S_b16[:, :], 0.0, inc="sinit")
    for i_ in range(2):
        mset("vector", v_tm[i_][:, :].rearrange("p (q v) -> p q v", v=KV)[:, :, 64:65], 1.0)
    mset("vector", st2[1][0:1, 9:10], 0.0, inc="pv")

    p.wait("tensor", "pv", PREP_W)
    p.wait("gpsimd", "pv", PREP_W)
    def front(c):
        bi = c % 2
        st = st2[bi]
        if c >= 2:
            p.wait("scalar", "v1", mval("red", c - 2))
            p.wait("scalar", "sq", mval("xn1tm", c - 2))
        XL = p.op("scalar", lambda e, o=xbuf[bi][:, :],
                  i=x_h[c * 128:(c + 1) * 128, :]: e.dma_start(o, i),
                  inc=f"xl{bi}", dma=True)
        mark("xl", c, XL)
        p.wait("vector", f"xl{bi}", XL)
        red("vector", st[:, SUM:SUM + 1], xbuf[bi][:, :], ALU.add, inc="v1")
        mark("red", c, p.val("v1"))
        p.wait("scalar", f"xl{bi}", XL)
        act(scr[:, :], xbuf[bi][:, :], AF.Square, accum=st[:, SQ:SQ + 1], inc="sq")
        mark("sqa", c, p.val("sq"))
        p.wait("vector", "sq", mval("sqa", c))
        ts("vector", st[:, M:M + 1], st[:, SUM:SUM + 1], INV_D, None, ALU.mult)
        tt("vector", st[:, VAR:VAR + 1], st[:, M:M + 1], st[:, M:M + 1], ALU.mult)
        stt("vector", st[:, V1:V1 + 1], st[:, SQ:SQ + 1], INV_D, st[:, VAR:VAR + 1],
            ALU.mult, ALU.subtract, inc="v1")
        mark("fst", c, p.val("v1"))

    def mid(c):
        g, cg = c // GSZ, c % GSZ
        bi = c % 2
        st = st2[bi]
        p.wait("scalar", "v1", mval("fst", c))
        act(st[:, R2:R2 + 1], st[:, V1:V1 + 1], AF.Ln, bias=st[:, EPS:EPS + 1])
        act(st[:, R_:R_ + 1], st[:, R2:R2 + 1], AF.Exp, scale=-0.5, inc="sq")
        p.wait("vector", "sq", p.val("sq"))
        tt("vector", st[:, VAR:VAR + 1], st[:, R_:R_ + 1], st[:, R_:R_ + 1], ALU.mult)
        tt("vector", st[:, R2:R2 + 1], st[:, VAR:VAR + 1], st[:, V1:V1 + 1],
           ALU.mult, inc="v1")
        p.wait("scalar", "v1", p.val("v1"))
        act(st[:, VAR:VAR + 1], st[:, R2:R2 + 1], AF.Ln, bias=st[:, EPS:EPS + 1])
        act(st[:, R2:R2 + 1], st[:, VAR:VAR + 1], AF.Exp, scale=-0.5, inc="sq")
        p.wait("vector", "sq", p.val("sq"))
        ts("vector", st[:, RR:RR + 1], st[:, M:M + 1], st[:, R2:R2 + 1], st[:, R_:R_ + 1],
           ALU.mult, ALU.mult)
        ts("vector", st[:, MB1:MB1 + 1], st[:, RR:RR + 1], -1.0, None, ALU.mult)
        ts("vector", st[:, RR:RR + 1], st[:, R_:R_ + 1], st[:, R2:R2 + 1], None,
           ALU.mult)
        ts("vector", st[:, MB:MB + 1], st[:, M:M + 1], st[:, R_:R_ + 1], -1.0,
           ALU.mult, ALU.mult, inc="v1")
        MBOK = p.val("v1")
        p.wait("scalar", "v1", MBOK)
        if c >= 2:
            p.wait("scalar", "v1", mval("amr", c - 2))
            p.wait("scalar", f"xnsp{bi}", mval("xnsp", c - 2))
        act(xnbuf[bi][:, :], xbuf[bi][:, :], AF.Identity, scale=st[:, R_:R_ + 1],
            bias=st[:, MB:MB + 1], inc="sq")
        mark("xnok", c, p.val("sq"))
        if c >= 2:
            p.wait("scalar", f"x1T{bi}", mval("xn1T", c - 2))
        act(xn1_tm[bi][:, :], xbuf[bi][:, :], AF.Identity, scale=st[:, RR:RR + 1],
            bias=st[:, MB1:MB1 + 1], inc="sq")
        mark("xn1tm", c, p.val("sq"))
        p.wait("gpsimd", "sq", p.val("sq"))
        mark("xnsp", c, dma(xn_sp[c * 128:(c + 1) * 128, :], xnbuf[bi][:, :],
                            inc=f"xnsp{bi}"))
        p.wait("sync", "sq", mval("xn1tm", c))
        if g >= 1 and cg == 0:
            p.wait("sync", "zg", mval("zg", g - 1))
            p.wait("sync", f"x1sp{(g - 1) % 2}", mval("xn1sp", g - 1))
        mark("xn1T", c,
             dmaT(S["xn1_g0"][:, :].rearrange("p (m r) -> p m r", r=512)
                  [:, :, cg * 128:(cg + 1) * 128], xn1_tm[bi][:, :],
                  inc=f"x1T{bi}"))

    def tail1(c):
        g, cg = c // GSZ, c % GSZ
        bi = c % 2
        gb = g % 2
        st = st2[bi]
        p.wait("vector", "sq", mval("xnok", c))
        red("vector", st[:, AM:AM + 1], xnbuf[bi][:, :], ALU.max, absval=True)
        ts("vector", st[:, XS:XS + 1], st[:, AM:AM + 1], 1e-6, 1.0 / 127.0, ALU.max,
           ALU.mult)
        rcp("vector", st[:, RS:RS + 1], st[:, XS:XS + 1], inc="v1")
        mark("amr", c, p.val("v1"))
        ts("vector", tbuf[bi][:, :], xnbuf[bi][:, :], st[:, RS:RS + 1], MR,
           ALU.mult, ALU.add)
        if c >= 2:
            p.wait("vector", f"xqT{bi}", mval("xqT", c - 2))
        ts("vector", xq_tm[bi][:, :], tbuf[bi][:, :], MR, None, ALU.subtract,
           inc="v1")
        mark("xqv", c, p.val("v1"))
        p.wait("sync", "v1", mval("xqv", c))
        if c >= 2:
            p.wait("sync", "Ae" if c % 2 == 0 else "Be", mval("qffn_ev", c - 2))
        mark("xqT", c, dmaT(S[f"xq_fm{bi}"][:, :].rearrange("p (m r) -> p m r", m=8),
                            xq_tm[bi][:, :], inc=f"xqT{bi}"))
        reg = "A" if c % 2 == 0 else "B"
        preg = PREG[reg]
        acquire(reg)
        p.wait("tensor", f"xqT{bi}", mval("xqT", c))
        p.wait("tensor", "wqTd", WQT)
        for t in range(8):
            for h2 in range(2):
                mm(preg[:, h2 * 512:(h2 + 1) * 512], xq_fm[bi][:, t * 128:(t + 1) * 128],
                   tiles(wqt_s)[:, t, h2 * 512:(h2 + 1) * 512],
                   start=(t == 0), stop=(t == 7),
                   inc=(reg + "f" if (t == 7 and h2 == 1) else None))
        nfill[reg] += 1
        mark("qfmm", c, p.val(reg + "f"))

        if cg == GSZ - 1:
            p.wait("tensor", f"x1T{c % 2}", mval("xn1T", c))
            p.wait("tensor", f"x1T{(c + 1) % 2}", mval("xn1T", c - 1))
            acquire("D")
            for t in range(8):
                mm(pD[0:16, 0:512], tiles(decw_s)[:, t, :], tiles(xn1_g[gb])[:, t, :],
                   start=(t == 0), stop=(t == 7), inc=("Df" if t == 7 else None))
            nfill["D"] += 1
            p.wait("vector", "Df", p.val("Df"))
            if g >= 1:
                p.wait("vector", "Ce", mval("ztr", g - 1))
            ts("vector", z_g[:, :], pD[0:16, 0:512], decbe_c[:, 0:1], None, ALU.add,
               inc="De")
            mark("zgv", g, p.val("De"))
            p.wait("tensor", "De", p.val("De"))
            acquire("C")
            for q in range(GSZ):
                tp(pC[:, q * 16:(q + 1) * 16], z_g[:, q * 128:(q + 1) * 128],
                   eye16_s[:, :], inc=("Cf" if q == GSZ - 1 else None))
            nfill["C"] += 1
            p.wait("vector", "Cf", p.val("Cf"))
            cp("vector", ztm[:, g * 64:(g + 1) * 64], pC[:, 0:64], inc="Ce")
            mark("ztr", g, p.val("Ce"))
            mset("vector", st2[0][0:1, 9:10], 0.0, inc="zg")
            mark("zg", g, p.val("zg"))
            p.wait("sync", f"x1T{c % 2}", mval("xn1T", c))
            p.wait("sync", f"x1T{(c + 1) % 2}", mval("xn1T", c - 1))
            mark("xn1sp", g, p.op("sync", lambda e, o=xn1_sp[g],
                                  i=xn1_g[gb][:, :]: e.dma_start(o, i),
                                  inc=f"x1sp{gb}", dma=True))

    def tail2(c):
        bi = c % 2
        st = st2[bi]
        reg = "A" if c % 2 == 0 else "B"
        preg = PREG[reg]
        p.wait("vector", reg + "f", mval("qfmm", c))
        if c >= 1:
            p.wait("vector", "qfsp", mval("qfsp", c - 1))
        stt("vector", qf_buf[bi][:, :], preg[:, :], st[:, XS:XS + 1], wst_s[:, :],
            ALU.mult, ALU.mult, inc=reg + "e")
        mark("qffn_ev", c, p.val(reg + "e"))
        p.wait("sync", reg + "e", p.val(reg + "e"))
        mark("qfsp", c, p.op("sync", lambda e,
                             o=qffn_sp[c * 128:(c + 1) * 128, :],
                             i=qf_buf[bi][:, :]: e.dma_start(o, i),
                             inc="qfsp", dma=True))

    for i_ in range(nch1a + 3):
        if i_ < nch1a:
            front(i_)
        if 2 <= i_ < nch1a + 2:
            tail1(i_ - 2)
        if 1 <= i_ < nch1a + 1:
            mid(i_ - 1)
        if i_ >= 3:
            tail2(i_ - 3)

    LAMTAB = {}
    if stage >= 2:
        p.wait("vector", "Ce", p.val("Ce"))
        mset("vector", st2[0][:, 10:11], 1.0)
        bitand("vector", lam_t1[:, :], ztm[:, :], 0x7FFFFFFF, inc="v2")
        p.wait("scalar", "v2", p.val("v2"))
        act(lam_t2[:, :], lam_t1[:, :], AF.Exp, scale=-1.0, inc="sc2")
        act(g_f32[:, :], lam_t2[:, :], AF.Ln, bias=st2[0][:, 10:11], inc="sc2")
        p.wait("vector", "sc2", p.val("sc2") - 1)
        ts("vector", lam_t1[:, :], ztm[:, :], 0.0, None, ALU.min)
        p.wait("vector", "sc2", p.val("sc2"))
        stt("vector", lam_t2[:, :], g_f32[:, :], -1.0, lam_t1[:, :], ALU.mult,
            ALU.add)
        ts("vector", g_f32[:, :], lam_t2[:, :], LOG1EM8, None, ALU.max)
        cp("vector", gh_tm[:, :], g_f32[:, :])
        tt("vector", lam_t2[:, :], g_f32[:, :], gh_tm[:, :], ALU.subtract)
        cp("vector", gl_tm[:, :], lam_t2[:, :], inc="v2")
        GOK = p.val("v2")
        p.wait("tensor", "v2", GOK)
        for r, lhs, outp, tab in (("A", U_s, pA, f_all), ("B", SL_s, pB, e_all)):
            acquire(r)
            mm(PREG[r][:, 0:512], lhs[:, :], gh_tm[:, :], start=True, stop=False)
            mm(PREG[r][:, 0:512], lhs[:, :], gl_tm[:, :], start=False, stop=True,
               inc=r + "f")
            nfill[r] += 1
            p.wait("scalar", r + "f", p.val(r + "f"))
            act(tab[:, :], PREG[r][:, 0:512], AF.Exp, inc=r + "e")
            LAMTAB[r] = p.val(r + "e")
        acquire("C")
        mm(pC[:, 0:512], ones128_s[:, :], gh_tm[:, :], start=True, stop=False)
        mm(pC[:, 0:512], ones128_s[:, :], gl_tm[:, :], start=False, stop=True,
           inc="Cf")
        nfill["C"] += 1
        p.wait("scalar", "Cf", p.val("Cf"))
        act(Lam_all[:, :], pC[:, 0:512], AF.Exp, inc="Ce")
        LAMTAB["C"] = p.val("Ce")


    mk2 = {}
    def mark2(name, c, sem, val):
        mk2[(name, c)] = (sem, val)
    def wmark(engine, name, c):
        if (name, c) in mk2:
            sem, val = mk2[(name, c)]
            p.wait(engine, sem, val)

    if stage >= 3:
        p.wait("vector", "Ae", LAMTAB["A"])
        p.wait("vector", "Be", LAMTAB["B"])
        p.wait("vector", "Ce", LAMTAB["C"])
        p.wait("gpsimd", "Ce", LAMTAB["C"])
        def _emit_diff_only(c, cg, bi):
            for w in range(2):
                regd = "A" if w == 0 else "B"
                pregd = PREG[regd]
                hh0 = w * 8
                acquire(regd)
                for hh in range(8):
                    blk_ = pregd[:, hh * 128:(hh + 1) * 128]
                    mm(blk_, negU_s[:, :], U_s[:, :], start=True, stop=True,
                       inc=(regd + "f" if hh == 7 else None))
                nfill[regd] += 1
                p.wait("vector", regd + "f", p.val(regd + "f"))
                cp("vector", dabs[:, :], pregd[:, :], inc=regd + "e")

        for g in range(ng1c):
            c0 = g * GSZ
            gp_ = g % 2
            if g >= 1:
                wmark("scalar", "lm", c0 - 1)
                p.wait("scalar", f"phq{(g - 1) % 2}", mval("phq", g - 1))
            else:
                p.wait("scalar", "De", mval("zgv", NG - 1))
                p.wait("scalar", f"x1sp{(NG - 1) % 2}", mval("xn1sp", NG - 1))
            XLG = p.op("scalar", lambda e, o=tiles(xn1_g[0]),
                       i=xn1_sp[g].rearrange("p (t n) -> p t n", t=8):
                       e.dma_start(o, i), inc=f"x1ld{gp_}", dma=True)
            p.wait("tensor", f"x1ld{gp_}", XLG)

            RROT = ("A", "B", "C", "D")
            sttv = {}
            for wi, (which, w_s, bcol) in enumerate(
                    (("q", qw_s, qbe_c), ("k", kw_s, kbe_c))):
                for dt in range(8):
                    di = wi * 8 + dt
                    reg = RROT[dt % 4]
                    preg = PREG[reg]
                    acquire(reg)
                    for t in range(8):
                        mm(preg[:, 0:512], tiles(w_s)[:, t, dt * 128:(dt + 1) * 128],
                           tiles(xn1_g[0])[:, t, :], start=(t == 0), stop=(t == 7),
                           inc=(reg + "f" if t == 7 else None))
                    nfill[reg] += 1
                    p.wait("scalar", reg + "f", p.val(reg + "f"))
                    if di >= 2:
                        p.wait("scalar", RROT[(di - 2) % 4] + "e", sttv[di - 2])
                    act(pq_r[dt % 2][:, :], preg[:, 0:512], AF.Relu,
                        bias=bcol[:, dt:dt + 1])
                    act(pq_e[dt % 2][:, :], preg[:, 0:512], AF.Exp,
                        bias=bcol[:, dt:dt + 1], inc="sc3")
                    p.wait("vector", "sc3", p.val("sc3"))
                    if g >= 1 and dt == 0 and which == "q":
                        wmark("vector", "lm", c0 - 1)
                        p.wait("vector", f"phq{(g - 1) % 2}", mval("phq", g - 1))
                    dst = phiq_g[0] if which == "q" else phik_fm[0]
                    stt("vector", tiles(dst)[:, dt, :], pq_e[dt % 2][:, :], 1.0,
                        pq_r[dt % 2][:, :], ALU.min, ALU.add, inc=reg + "e")
                    sttv[di] = p.val(reg + "e")
                if which == "q" and sub3 != 27:
                    p.wait("sync", "Ae", p.val("Ae"))
                    p.wait("sync", "Be", p.val("Be"))
                    p.wait("sync", "Ce", p.val("Ce"))
                    p.wait("sync", "De", p.val("De"))
                    mark("phq", g, p.op("sync", lambda e,
                                        o=phiq_sp[g].rearrange("p (t n) -> p t n", t=8),
                                        i=tiles(phiq_g[0]): e.dma_start(o, i),
                                        inc=f"phq{gp_}", dma=True))
            PKT = 0
            if sub3 not in (26, 31, 32):
                p.wait("sync", "Ae", p.val("Ae"))
                p.wait("sync", "Be", p.val("Be"))
                p.wait("sync", "Ce", p.val("Ce"))
                p.wait("sync", "De", p.val("De"))
                if g >= 1:
                    wmark("sync", "kef", c0 - 1)
                PKT = dmaT(phik_tmg[:, :].rearrange("p (m r) -> p m r", m=32),
                           phik_fm[0][:, :], inc=f"pkT{gp_}")

            for cg in range(GSZ):
                if sub3 < 2:
                    continue
                SKIPVK = (sub3 == 28)
                VK31 = sub3 in (31, 32)
                c = c0 + cg
                bi = c % 2
                if SKIPVK:
                    _emit_diff_only(c, cg, bi)
                    continue
                acquire("C")
                for t in range(8):
                    for h2 in range(2):
                        mm(pC[:, h2 * 512:(h2 + 1) * 512],
                           tiles(xn1_g[0])[:, t, cg * 128:(cg + 1) * 128],
                           tiles(vw_s)[:, t, h2 * 512:(h2 + 1) * 512],
                           start=(t == 0), stop=(t == 7),
                           inc=("Cf" if (t == 7 and h2 == 1) else None))
                nfill["C"] += 1
                p.wait("vector", "Cf", p.val("Cf"))
                if c >= 2:
                    wmark("vector", "lm", c - 2)
                tt("vector",
                   v_tm[bi][:, :].rearrange("p (q v) -> p q v", v=KV)[:, :, 0:64],
                   pC[:, :].rearrange("p (q v) -> p q v", v=64),
                   vb_t[:, :].rearrange("p (q v) -> p q v", v=64), ALU.add, inc="Ce")
                VTM = p.val("Ce")

                if VK31 and sub3 == 31:
                    _emit_diff_only(c, cg, bi)
                    continue
                if sub3 not in (26, 31, 32):
                    p.wait("vector", f"pkT{gp_}", PKT)
                if c >= 1:
                    wmark("vector", "lm", c - 1)
                ktm4 = bass.AP(S["phik_tmg"], cg * 128,
                               [[32 * 128, 128], [512, 8], [64, 2], [1, 64]])
                e4 = e_all[:, c * 16:(c + 1) * 16].rearrange("p (a b) -> p a b", a=8) \
                    .unsqueeze(3).broadcast_to((128, 8, 2, 64))
                f4 = f_all[:, c * 16:(c + 1) * 16].rearrange("p (a b) -> p a b", a=8) \
                    .unsqueeze(3).broadcast_to((128, 8, 2, 64))
                ke4 = kef_e[:, :].rearrange("p (a b v) -> p a b v", a=8, b=2)
                kf4 = kef_f[:, :].rearrange("p (a b v) -> p a b v", a=8, b=2)
                tt("vector", ke4, ktm4, e4, ALU.mult)
                tt("vector", kf4, ktm4, f4, ALU.mult, inc="kefv")
                mark("kef", c, p.val("kefv"))
                mark2("kef", c, "kefv", p.val("kefv"))
                if VK31:
                    _emit_diff_only(c, cg, bi)
                    continue

                if sub3 < 3 and sub3 not in (22, 23, 24, 25, 26, 27, 29, 30, 31, 32):
                    continue
                for w in range(2):
                    regd = "A" if w == 0 else "B"
                    pregd = PREG[regd]
                    hh0 = w * 8
                    acquire(regd)
                    for pos, hh in enumerate(PERM8):
                        col = c * 16 + hh0 + hh
                        blk_ = pregd[:, pos * 128:(pos + 1) * 128]
                        if sub3 == 29:
                            mm(blk_, negU_s[:, :], U_s[:, :], start=True, stop=True,
                               inc=(regd + "f" if pos == 7 else None))
                            continue
                        if sub3 == 30:
                            mm(blk_, gh_tm[:, col:col + 1].broadcast_to((128, 128)),
                               U_s[:, :], start=True, stop=True,
                               inc=(regd + "f" if pos == 7 else None))
                            continue
                        if sub3 in (22, 26, 27):
                            mm(blk_, negU_s[:, :],
                               gh_tm[:, col:col + 1].broadcast_to((128, 128)),
                               start=True, stop=True,
                               inc=(regd + "f" if pos == 7 else None))
                            continue
                        if sub3 == 23:
                            for si, src in enumerate((gh_tm, gl_tm)):
                                mm(blk_, src[:, col:col + 1].broadcast_to((128, 128)),
                                   U_s[:, :], start=(si == 0), stop=(si == 1),
                                   inc=(regd + "f" if (pos == 7 and si == 1) else None))
                            continue
                        for si, src in enumerate((gh_tm, gl_tm)):
                            mm(blk_, negU_s[:, :], src[:, col:col + 1].broadcast_to((128, 128)),
                               start=(si == 0), stop=False)
                        for si, src in enumerate((gh_tm, gl_tm)):
                            last = (pos == 7 and si == 1)
                            mm(blk_, src[:, col:col + 1].broadcast_to((128, 128)), U_s[:, :],
                               start=False, stop=(si == 1),
                               inc=(regd + "f" if last else None))
                    nfill[regd] += 1
                    if sub3 in (22, 23, 24, 26, 27, 29, 30):
                        p.wait("vector", regd + "f", p.val(regd + "f"))
                        cp("vector", dabs[:, :], pregd[:, :], inc=regd + "e")
                        continue
                    p.wait("scalar", regd + "f", p.val(regd + "f"))
                    if 2 * c + w >= 1:
                        wmark("scalar", "ptv", 2 * c + w - 1)
                    act(dabs[:, :], pregd[:, :], AF.Abs, inc=regd + "e")
                    act(dmat[:, :], dabs[:, :], AF.Exp, scale=-1.0, inc="dm")
                    DM = p.val("dm")
                    if sub3 in (22, 23, 24, 25, 26, 27, 29, 30):
                        continue
                    acquire(regd)
                    for pos, hh in enumerate(PERM8):
                        h = hh0 + hh
                        dt, half = h // 2, (h % 2) * 64
                        mm(pregd[:, pos * 128:(pos + 1) * 128],
                           tiles(phik_fm[0])[half:half + 64, dt, cg * 128:(cg + 1) * 128],
                           tiles(phiq_g[0])[half:half + 64, dt, cg * 128:(cg + 1) * 128],
                           start=True, stop=True,
                           inc=(regd + "f" if pos == 7 else None))
                    nfill[regd] += 1
                    p.wait("vector", regd + "f", p.val(regd + "f"))
                    p.wait("vector", "dm", DM)
                    if c >= 1 and w == 0:
                        wmark("vector", "lm", c - 1)
                    if sub3 == 33:
                        cp("vector", pt_sb[:, w * 1024:(w + 1) * 1024], pregd[:, :],
                           inc=regd + "e")
                    else:
                        tt("vector", pt_sb[:, w * 1024:(w + 1) * 1024], pregd[:, :],
                           dmat[:, :], ALU.mult, inc=regd + "e")
                    mark2("ptv", 2 * c + w, regd + "e", p.val(regd + "e"))

                if sub3 < 4 or sub3 >= 22:
                    continue
                for w in range(2):
                    acquire("C")
                    wmark("tensor", "ptv", 2 * c + w)
                    for pos, hh in enumerate(PERM8):
                        h = w * 8 + hh
                        mm(pC[:, pos * 128:pos * 128 + KV],
                           pt_sb[:, (w * 8 + pos) * 128:(w * 8 + pos + 1) * 128],
                           v_tm[bi][:, h * KV:(h + 1) * KV], start=True, stop=True,
                           inc=("Cf" if pos == 7 else None))
                    nfill["C"] += 1
                    acquire("D")
                    if c >= 1 and w == 0:
                        wmark("tensor", "sb", c - 1)
                    for pos, hh in enumerate(PERM8):
                        h = w * 8 + hh
                        dt, half = h // 2, (h % 2) * 64
                        mm(pD[:, pos * 128:pos * 128 + KV],
                           tiles(phiq_g[0])[half:half + 64, dt, cg * 128:(cg + 1) * 128],
                           S_b16[half:half + 64, dt * KV:(dt + 1) * KV], start=True,
                           stop=True, inc=("Df" if pos == 7 else None))
                    nfill["D"] += 1
                    p.wait("vector", "Cf", p.val("Cf"))
                    p.wait("vector", "Df", p.val("Df"))
                    fcol0 = c * 16 + w * 8
                    f4 = bass.AP(S["f_all"], fcol0, [[16 * NCH, 128], [1, 2], [2, 4], [0, KV]])
                    pD4 = pD[:, :].rearrange("p (b q r) -> p b q r", b=2, q=4)[:, :, :, 0:KV]
                    pC4 = pC[:, :].rearrange("p (b q r) -> p b q r", b=2, q=4)[:, :, :, 0:KV]
                    fi4 = fint_t[:, :].rearrange("p (b q v) -> p b q v", b=2, q=4)
                    tt("vector", fi4, pD4, f4, ALU.mult, inc="De")
                    if c >= 1 and w == 0:
                        p.wait("vector", f"osp{1 - bi}", mval("opsp", c - 1))
                    ov4 = bass.AP(S["outp_sb0"], w * 8 * KV,
                                  [[16 * KV, 128], [KV, 2], [2 * KV, 4], [1, KV]])
                    tt("vector", ov4, pC4, fi4, ALU.add, inc="Ce")

                if sub3 < 5 or sub3 >= 22:
                    continue
                for w in range(2):
                    acquire("A")
                    acquire("B")
                    if w == 0:
                        wmark("tensor", "kef", c)
                        p.wait("tensor", "Ce", VTM)
                    for mat, (kefm, regd) in enumerate(((kef_e, "A"), (kef_f, "B"))):
                        pregd = PREG[regd]
                        for half in range(2):
                            r0 = 64 * half
                            for j in range(4):
                                h = w * 8 + 2 * j + half
                                last = (half == 1 and j == 3)
                                p.op("tensor", lambda e,
                                     o=pregd[r0:r0 + 64,
                                             half * 512 + j * 128:half * 512 + j * 128 + KV],
                                     l=kefm[:, h * 64:(h + 1) * 64],
                                     r=v_tm[bi][:, h * KV:(h + 1) * KV], tp_=(0, r0):
                                     e.matmul(o, l, r, start=True, stop=True,
                                              tile_position=tp_),
                                     inc=(regd + "f" if last else None))
                    nfill["A"] += 1
                    nfill["B"] += 1
                    if w == 1:
                        mark2("lm", c, "Bf", p.val("Bf"))
                    cur, nxt = S_f32[c % 2], S_f32[(c + 1) % 2]
                    dt0 = 4 * w
                    if w == 0:
                        if c >= 1:
                            wmark("gpsimd", "sb", c - 1)
                        else:
                            p.wait("gpsimd", "sinit", p.val("sinit"))
                        hzfence("gpsimd")
                        for half in range(2):
                            rows = slice(64 * half, 64 * half + 64)
                            Lam8h = bass.AP(S["Lam_all"],
                                            64 * half * (16 * NCH) + c * 16 + half,
                                            [[16 * NCH, 64], [2, 8], [0, KV]])
                            tt("gpsimd", sc_tmp[rows, :]
                               .rearrange("p (q v) -> p q v", v=KV),
                               cur[rows, :].rearrange("p (q v) -> p q v", v=KV),
                               Lam8h, ALU.mult,
                               inc=("smul" if half == 1 else None))
                    p.wait("vector", "Af", p.val("Af"))
                    p.wait("vector", "Bf", p.val("Bf"))
                    p.wait("vector", "smul", p.val("smul"))
                    for half in range(2):
                        rows = slice(64 * half, 64 * half + 64)
                        tt("vector", nxt[rows, dt0 * KV:(dt0 + 4) * KV]
                           .rearrange("p (q v) -> p q v", v=KV),
                           pA[rows, half * 512:half * 512 + 512]
                           .rearrange("p (q r) -> p q r", q=4)[:, :, 0:KV],
                           sc_tmp[rows, w * 4 * KV:(w * 4 + 4) * KV]
                           .rearrange("p (q v) -> p q v", v=KV), ALU.add,
                           inc=("Ae" if half == 1 else None))
                        if c >= 1 and w == 0 and half == 0:
                            p.wait("vector", f"wsp{1 - bi}", mval("wsp", c - 1))
                        cp("vector",
                           w_stg[0][rows, dt0 * KV:(dt0 + 4) * KV]
                           .rearrange("p (q v) -> p q v", v=KV),
                           pB[rows, half * 512:half * 512 + 512]
                           .rearrange("p (q r) -> p q r", q=4)[:, :, 0:KV],
                           inc=("Be" if half == 1 else None))
                cp("vector", S_b16[:, :], nxt[:, :], inc="sb")
                mark2("sb", c, "sb", p.val("sb"))
                p.wait("sync", "Ce", p.val("Ce"))
                p.wait("sync", "Ae", p.val("Ae"))
                p.wait("sync", "Be", p.val("Be"))
                mark("opsp", c, p.op("sync", lambda e, o=op_sp[c],
                                     i=outp_sb[0][:, :]: e.dma_start(o, i),
                                     inc=f"osp{bi}", dma=True))
                mark("wsp", c, p.op("sync", lambda e, o=w_sp[c],
                                    i=w_stg[0][:, :]: e.dma_start(o, i),
                                    inc=f"wsp{bi}", dma=True))


    def engine_fence(waits):
        for e in Prog.ENGINES:
            for nm, v in waits:
                p.wait(e, nm, v)

    if stage >= 4:
        FIN1 = [(nm, p.val(nm)) for nm in
                ("Ae", "Be", "Ce", "De", "Af", "Bf", "Cf", "Df", "sb", "kefv",
                 "sc3", "dm", "v1", "v2", "sq", "qfsp",
                 "osp0", "osp1", "wsp0", "wsp1", "phq0", "phq1",
                 "x1ld0", "x1ld1", "pkT0", "pkT1", "xqT0", "xqT1", "x1T0", "x1T1",
                 "xnsp0", "xnsp1", "x1sp0", "x1sp1", "wqTd", "xl0", "xl1",
                 "smul", "sinit")]
        p1.close()
        p2a = ExitStack(); SB2 = mk_sb(p2a)
        ow_s = SB2("ow_s", (128, 8 * D), BF16)
        obt_s = SB2("obt_s", (128, D))
        phiq_2 = SB2("phiq_2", (128, 8 * 512), BF16)
        opld = [SB2(f"opld{i}", (128, 16 * KV)) for i in range(2)]
        wld = [SB2(f"wld{i}", (128, 8 * KV)) for i in range(2)]
        R_f32 = [SB2(f"R_f32{i}", (128, 8 * KV)) for i in range(2)]
        R_b16 = SB2("R_b16", (128, 8 * KV), BF16)
        rtmp = SB2("rtmp", (128, 8 * KV))
        bint_t = SB2("bint_t", (128, 16 * KV))
        xsc = SB2("xsc", (128, D))
        numf = SB2("numf", (128, 16 * KV))
        rden = SB2("rden", (128, 32))
        attn_tm = [SB2(f"attn_tm{i}", (128, D), BF16) for i in range(2)]
        attn_fm = [SB2(f"attn_fm{i}", (128, 8 * 128), BF16) for i in range(2)]
        xnld = [SB2(f"xnld{i}", (128, D)) for i in range(2)]
        xmid_b = [SB2(f"xmid_b{i}", (128, D)) for i in range(2)]
        engine_fence(FIN1)
        dma(tiles(ow_s), ow_h.rearrange("(t p) n -> p t n", p=128), inc="ld2")
        dma(obt_s[:, :], obt_h, inc="ld2")
        LD2 = p.val("ld2")
        obt16h = SB2("obt16h", (1, D), BF16)
        obt16l = SB2("obt16l", (1, D), BF16)
        mset("vector", R_f32[0][:, :], 0.0)
        mset("vector", R_f32[1][:, :], 0.0)
        mset("vector", R_b16[:, :], 0.0, inc="rb")
        mark2("rb", NCH, "rb", p.val("rb"))
        p.wait("vector", "ld2", LD2)
        cp("vector", obt16h[0:1, :], obt_s[0:1, :])
        tt("vector", xmid_b[0][0:1, :], obt_s[0:1, :], obt16h[0:1, :],
           ALU.subtract)
        cp("vector", obt16l[0:1, :], xmid_b[0][0:1, :], inc="ob16")
        p.wait("tensor", "ob16", p.val("ob16"))
        p.wait("tensor", "ld2", LD2)

        def oreg(c_):
            return "C" if c_ % 2 == 0 else "D"

        def emit_oproj(cx):
            bx = cx % 2
            r_ = oreg(cx)
            preg_o = PREG[r_]
            acquire(r_)
            p.wait("tensor", f"atT{bx}", mval("attT", cx))
            for t in range(8):
                for h2 in range(2):
                    mm(preg_o[:, h2 * 512:(h2 + 1) * 512],
                       attn_fm[bx][:, t * 128:(t + 1) * 128],
                       tiles(ow_s)[:, t, h2 * 512:(h2 + 1) * 512],
                       start=(t == 0), stop=False)
            for h2 in range(2):
                for oi, osrc in enumerate((obt16h, obt16l)):
                    mm(preg_o[:, h2 * 512:(h2 + 1) * 512],
                       ones128_s[0:1, 0:128], osrc[0:1, h2 * 512:(h2 + 1) * 512],
                       start=False, stop=(oi == 1),
                       inc=(r_ + "f" if (h2 == 1 and oi == 1) else None))
            nfill[r_] += 1
            mark2("om", cx, r_ + "f", p.val(r_ + "f"))
            mark2("om2", cx, r_ + "f", p.val(r_ + "f"))

        def emit_xmid(cx):
            bx = cx % 2
            r = oreg(cx)
            wmark("vector", "om2", cx)
            p.wait("vector", f"xnl{bx}", mval("xnl", cx))
            wmark("vector", "xms", cx + 2)
            tt("vector", xmid_b[bx][:, :], PREG[r][:, :], xnld[bx][:, :],
               ALU.add, inc=r + "e")
            mark2("xm", cx, r + "e", p.val(r + "e"))
            p.wait("gpsimd", r + "e", p.val(r + "e"))
            XMS = dma(xmid_sp[cx * 128:(cx + 1) * 128, :], xmid_b[bx][:, :],
                      inc=f"xms{bx}")
            mark2("xms", cx, f"xms{bx}", XMS)

        for c in range(NCH - 1, -1, -1):
            g, cg = c // GSZ, c % GSZ
            bi = c % 2
            if c + 2 <= NCH - 1:
                emit_xmid(c + 2)
            if cg == GSZ - 1:
                wmark("gpsimd", "bm", (g + 1) * GSZ)
                p.wait("gpsimd", f"phq{g % 2}", mval("phq", g))
                PH2 = dma(tiles(phiq_2), phiq_sp[g].rearrange("p (t n) -> p t n", t=8),
                          inc=f"p2q{g % 2}")
                mark("p2q", g, PH2)
            wmark("scalar", "nc", c + 2)
            wmark("scalar", "nc2", c + 2)
            wmark("scalar", "ncw", c + 2)
            p.wait("scalar", f"osp{bi}", mval("opsp", c))
            p.wait("scalar", f"wsp{bi}", mval("wsp", c))
            OPL = p.op("scalar", lambda e, o=opld[bi][:, :], i=op_sp[c]:
                       e.dma_start(o, i), inc=f"opl{bi}", dma=True)
            WL = p.op("scalar", lambda e, o=wld[bi][:, :], i=w_sp[c]:
                      e.dma_start(o, i), inc=f"wl{bi}", dma=True)
            wmark("sync", "xm", c + 2)
            p.wait("sync", f"xnsp{bi}", mval("xnsp", c))
            XNL = p.op("sync", lambda e, o=xnld[bi][:, :],
                       i=xn_sp[c * 128:(c + 1) * 128, :]: e.dma_start(o, i),
                       inc=f"xnl{bi}", dma=True)
            mark("xnl", c, XNL)

            p.wait("tensor", f"p2q{g % 2}", mval("p2q", g))
            wmark("tensor", "rb", c + 1)
            for w in range(2):
                regd = "A" if w == 0 else "B"
                pregd = PREG[regd]
                acquire(regd)
                for pos, hh in enumerate(PERM8):
                    h = w * 8 + hh
                    dt, half = h // 2, (h % 2) * 64
                    mm(pregd[:, pos * 128:pos * 128 + KV],
                       tiles(phiq_2)[half:half + 64, dt, cg * 128:(cg + 1) * 128],
                       R_b16[half:half + 64, dt * KV:(dt + 1) * KV], start=True,
                       stop=True, inc=(regd + "f" if pos == 7 else None))
                nfill[regd] += 1
                if w == 1:
                    mark2("bm", c, "Bf", p.val("Bf"))
            for w in range(2):
                regd = "A" if w == 0 else "B"
                pregd = PREG[regd]
                if w == 0:
                    wmark("scalar", "gn", c + 1)
                else:
                    wmark("scalar", "nc2", c + 1)
                p.wait("scalar", regd + "f", p.val(regd + "f"))
                for pos in range(8):
                    hh = PERM8[pos]
                    col = c * 16 + w * 8 + hh
                    act(bint_t[:, (w * 8 + hh) * KV:(w * 8 + hh + 1) * KV],
                        pregd[:, pos * 128:pos * 128 + KV], AF.Copy,
                        scale=e_all[:, col:col + 1],
                        inc=(regd + "e" if pos == 7 else None))
            cur, nxt = R_f32[c % 2], R_f32[(c + 1) % 2]
            wmark("gpsimd", "rb", c + 1)
            hzfence("gpsimd")
            for half in range(2):
                rows = slice(64 * half, 64 * half + 64)
                Lam8h = bass.AP(S["Lam_all"],
                                64 * half * (16 * NCH) + c * 16 + half,
                                [[16 * NCH, 64], [2, 8], [0, KV]])
                tt("gpsimd", rtmp[rows, :].rearrange("p (q v) -> p q v", v=KV),
                   cur[rows, :].rearrange("p (q v) -> p q v", v=KV),
                   Lam8h, ALU.mult, inc=("gmul" if half == 1 else None))
            wmark("gpsimd", "at", c + 1)
            p.wait("gpsimd", f"opl{bi}", OPL)
            p.wait("gpsimd", "Ae", p.val("Ae"))
            tt("gpsimd", numf[:, 0:8 * KV], opld[bi][:, 0:8 * KV],
               bint_t[:, 0:8 * KV], ALU.add, inc="gnum")
            mark2("nc", c, "gnum", p.val("gnum"))
            mark2("gn", c, "gnum", p.val("gnum"))
            p.wait("vector", f"wl{bi}", WL)
            p.wait("vector", "gmul", p.val("gmul"))
            for half in range(2):
                rows = slice(64 * half, 64 * half + 64)
                tt("vector", nxt[rows, :].rearrange("p (q v) -> p q v", v=KV),
                   wld[bi][rows, :].rearrange("p (q v) -> p q v", v=KV),
                   rtmp[rows, :].rearrange("p (q v) -> p q v", v=KV), ALU.add,
                   inc=("radd" if half == 1 else None))
            mark2("ncw", c, "radd", p.val("radd"))
            p.wait("vector", "Bf", p.val("Bf"))
            cp("vector", R_b16[:, :], nxt[:, :], inc="rb")
            mark2("rb", c, "rb", p.val("rb"))
            p.wait("vector", "Be", p.val("Be"))
            p.wait("vector", f"opl{bi}", OPL)
            tt("vector",
               numf[:, 8 * KV:16 * KV].rearrange("p (q v) -> p q v", v=KV),
               opld[bi][:, 8 * KV:16 * KV].rearrange("p (q v) -> p q v", v=KV),
               bint_t[:, 8 * KV:16 * KV].rearrange("p (q v) -> p q v", v=KV),
               ALU.add, inc="vnum")
            mark2("nc2", c, "vnum", p.val("vnum"))
            p.wait("vector", "gnum", p.val("gnum"))
            den = bass.AP(S["numf"], 64, [[16 * KV, 128], [KV, 16]])
            ts("vector", rden[:, 0:16], den, 1e-6, None, ALU.max)
            rcp("vector", rden[:, 16:32], rden[:, 0:16])
            if c <= NCH - 3:
                wmark("vector", "atT", c + 2)
            num3 = bass.AP(S["numf"], 0, [[16 * KV, 128], [KV, 16], [1, 64]])
            r3 = rden[:, 16:32].unsqueeze(2).broadcast_to((128, 16, 64))
            tt("vector", attn_tm[bi][:, :].rearrange("p (q v) -> p q v", v=64),
               num3, r3, ALU.mult, inc="atv")
            ATV = p.val("atv")
            mark2("at", c, "atv", ATV)
            p.wait("sync", "atv", ATV)
            if c <= NCH - 3:
                wmark("sync", "om", c + 2)
            ATT = dmaT(S[f"attn_fm{bi}"][:, :].rearrange("p (m r) -> p m r", m=8),
                       attn_tm[bi][:, :], inc=f"atT{bi}")
            mark2("atT", c, f"atT{bi}", ATT)
            mark("attT", c, ATT)
            if c + 1 <= NCH - 1:
                emit_oproj(c + 1)

        emit_oproj(0)
        emit_xmid(1)
        emit_xmid(0)

    if stage >= 5:
        FIN2 = [(nm, p.val(nm)) for nm in
                ("Ae", "Be", "Ce", "De", "Af", "Bf", "Cf", "Df", "rb", "atv",
                 "atT0", "atT1", "xms0", "xms1", "opl0", "opl1", "wl0", "wl1",
                 "xnl0", "xnl1", "p2q0", "p2q1", "ld2", "gnum", "vnum",
                 "gmul", "radd", "ob16")]
        if debug:
            dma(z_dbg, ztm[:, :], inc="dbg")
            dma(f_dbg, f_all[:, :], inc="dbg")
            dma(e_dbg, e_all[:, :], inc="dbg")
            dma(L_dbg, Lam_all[:, :], inc="dbg")
            dma(g_dbg, g_f32[:, :], inc="dbg")
            p.wait("gpsimd", "dbg", p.val("dbg"))
        p2a.close()
        lamS.close()
        p2b = ExitStack(); SB3 = mk_sb(p2b)
        f1w_s = SB3("f1w_s", (128, 8 * FFN), BF16)
        f2w_s = SB3("f2w_s", (128, 32 * D), BF16)
        row2 = SB3("row2", (1, 1024))
        f1be_c = SB3("f1be_c", (128, 32))
        h_fmb = SB3("h_fmb", (128, 32 * 512), BF16)
        x2fmb = SB3("x2fmb", (128, 4 * 1024), BF16)
        xmld = [SB3(f"xmld{i}", (128, D)) for i in range(2)]
        x2_tm = [SB3(f"x2_tm{i}", (128, D), BF16) for i in range(4)]
        obuf = [SB3(f"obuf{i}", (128, D)) for i in range(2)]
        xqb16h = SB3("xqb16h", (1, D), BF16)
        xqb16l = SB3("xqb16l", (1, D), BF16)
        engine_fence(FIN2)
        dma(tiles(f1w_s, 8), f1w_h.rearrange("(t p) n -> p t n", p=128), inc="ld3")
        dma(obuf[0][:, :], f2bt_h, inc="ld3")
        dma(obuf[1][:, :], qfbt_h, inc="ld3")
        LD3 = p.val("ld3")
        dma(f2w_s[:, :].rearrange("p (t n) -> p t n", t=32),
            f2w_h.rearrange("(t p) n -> p t n", p=128), inc="ldf2")
        LDF2 = p.val("ldf2")
        p.wait("vector", "ld3", LD3)
        tt("vector", row2[0:1, :], obuf[0][0:1, :], obuf[1][0:1, :], ALU.add)
        cp("vector", xqb16h[0:1, :], row2[0:1, :])
        tt("vector", obuf[0][0:1, :], row2[0:1, :], xqb16h[0:1, :],
           ALU.subtract)
        cp("vector", xqb16l[0:1, :], obuf[0][0:1, :], inc="v3")
        def prep2b():
            p.wait("tensor", "ld3", LD3)
            for q8 in range(4):
                q0 = q8 * 1024
                QL = dma(row2[0:1, :], f1b_h[0:1, q0:q0 + 1024], inc=f"r2l{q8 % 2}")
                p.wait("vector", f"r2l{q8 % 2}", QL)
                acquire("D")
                for h2 in range(2):
                    for t in range(8):
                        mm(pD[0:1, h2 * 512:(h2 + 1) * 512], ln2b16[:, t:t + 1],
                           tiles(f1w_s, 8)[:, t, q0 + h2 * 512:q0 + (h2 + 1) * 512],
                           start=(t == 0), stop=(t == 7),
                           inc=("Df" if (t == 7 and h2 == 1) else None))
                nfill["D"] += 1
                p.wait("vector", "Df", p.val("Df"))
                tt("vector", row2[0:1, :], pD[0:1, 0:1024], row2[0:1, :], ALU.add,
                   inc="De")
                p.wait("tensor", "De", p.val("De"))
                acquire("D")
                for t in range(8):
                    tp(pD[:, t:t + 1], row2[0:1, t * 128:(t + 1) * 128], eye1_s[0:1, 0:1],
                       inc=("Df" if t == 7 else None))
                nfill["D"] += 1
                p.wait("vector", "Df", p.val("Df"))
                cp("vector", f1be_c[:, q8 * 8:(q8 + 1) * 8], pD[:, 0:8], inc="De")
                p.wait("tensor", "De", p.val("De"))
                p.wait("gpsimd", "De", p.val("De"))
            for t in range(8):
                wv = tiles(f1w_s, 8)[:, t, :]
                hslice = h_fmb[:, (t % 2) * FFN:(t % 2 + 1) * FFN]
                if t >= 2:
                    p.wait("vector", "fsc", p.val("fsc"))
                ts("vector", hslice, wv, ln2w_s[:, t:t + 1], None, ALU.mult,
                   inc="fsv")
                p.wait("scalar", "fsv", p.val("fsv"))
                act(wv, hslice, AF.Copy, inc="fsc")
            p.wait("tensor", "fsc", p.val("fsc"))
            mset("vector", st2[0][0:1, 12:13], 0.0, inc="v3")


        def emit_tail(cc):
            bi = cc % 2
            r_ = "C" if bi == 0 else "D"
            wmark("scalar", "f2f", cc)
            wmark("scalar", "outw", cc - 2)
            act(obuf[bi][:, :], PREG[r_][:, :], AF.Copy, inc=r_ + "e")
            EV = p.val(r_ + "e")
            p.wait("gpsimd", r_ + "e", EV)
            for asrc in (xmid_sp[cc * 128:(cc + 1) * 128, :],
                         x_h[cc * 128:(cc + 1) * 128, :],
                         qffn_sp[cc * 128:(cc + 1) * 128, :]):
                AXW = p.op("gpsimd", lambda e, o=obuf[bi][:, :], i=asrc:
                           e.dma_start(o, i, accum_op=ALU.add),
                           inc=f"ax{bi}", dma=True)
                p.wait("gpsimd", f"ax{bi}", AXW)
            mark2("outw", cc, f"outw{bi}",
                  dma(out_h[cc * 128:(cc + 1) * 128, :], obuf[bi][:, :],
                      inc=f"outw{bi}"))

        def ln2blk(sI):
            for q in range(4):
                cc = 4 * sI + q
                bi = cc % 2
                st = st2[bi]
                wmark("gpsimd", "lnd", cc - 2)
                wmark("gpsimd", "xms", cc)
                XML = dma(xmld[bi][:, :], xmid_sp[cc * 128:(cc + 1) * 128, :],
                          inc=f"xml{bi}")
                p.wait("vector", f"xml{bi}", XML)
                red("vector", st[:, SUM:SUM + 1], xmld[bi][:, :], ALU.add,
                    inc="v3")
                p.wait("scalar", "v3", p.val("v3"))
                act(scr[:, :], xmld[bi][:, :], AF.Square,
                    accum=st[:, SQ:SQ + 1], inc="sq3")
                p.wait("vector", "sq3", p.val("sq3"))
                ts("vector", st[:, M:M + 1], st[:, SUM:SUM + 1], INV_D, None,
                   ALU.mult)
                tt("vector", st[:, VAR:VAR + 1], st[:, M:M + 1], st[:, M:M + 1],
                   ALU.mult)
                stt("vector", st[:, R_:R_ + 1], st[:, SQ:SQ + 1], INV_D,
                    st[:, VAR:VAR + 1], ALU.mult, ALU.subtract, inc="v3")
                p.wait("scalar", "v3", p.val("v3"))
                act(st[:, R2:R2 + 1], st[:, R_:R_ + 1], AF.Ln,
                    bias=st[:, EPS:EPS + 1])
                act(st[:, R_:R_ + 1], st[:, R2:R2 + 1], AF.Exp, scale=-0.5,
                    inc="sq3")
                p.wait("vector", "sq3", p.val("sq3"))
                wmark("vector", "x2T", cc - 4)
                ts("vector", x2_tm[cc % 4][:, :], xmld[bi][:, :], st[:, M:M + 1],
                   st[:, R_:R_ + 1], ALU.subtract, ALU.mult, inc="v3")
                mark2("lnd", cc, "v3", p.val("v3"))
                mark2("x2v", cc, "v3", p.val("v3"))

        ln2blk(0)
        prep2b()
        for sI in range(NCH // 4):
            if sI >= 1:
                ln2blk(sI)
            if sI >= 1:
                for q in range(4):
                    emit_tail(4 * (sI - 1) + q)
            for q in range(4):
                cc = 4 * sI + q
                wmark("sync", "x2v", cc)
                if sI >= 1 and q == 0:
                    wmark("sync", "f1m", sI - 1)
                X2T = dmaT(x2fmb[:, q * 1024:(q + 1) * 1024]
                           .rearrange("p (m r) -> p m r", m=8),
                           x2_tm[cc % 4][:, :], inc="xtb")
            for qq in range(4):
                mark2("x2T", 4 * sI + qq, "xtb", X2T)
            p.wait("tensor", "xtb", X2T)
            for fp in range(16):
                regd = "A" if fp % 2 == 0 else "B"
                pregd = PREG[regd]
                acquire(regd)
                for fi in range(2):
                    ft = 2 * fp + fi
                    for t in range(8):
                        mov = bass.AP(S["x2fmb"], t * 128,
                                      [[4096, 128], [1024, 4], [1, 128]])
                        mm(pregd[:, fi * 512:(fi + 1) * 512],
                           tiles(f1w_s, 8)[:, t, ft * 128:(ft + 1) * 128],
                           mov, start=(t == 0), stop=(t == 7),
                           inc=(regd + "f" if (fi == 1 and t == 7) else None))
                nfill[regd] += 1
                if fp >= 14:
                    mark2("f1m", sI, regd + "f", p.val(regd + "f"))
                p.wait("scalar", regd + "f", p.val(regd + "f"))
                for fi in range(2):
                    ft = 2 * fp + fi
                    act(h_fmb[:, ft * 512:(ft + 1) * 512],
                        pregd[:, fi * 512:(fi + 1) * 512], AF.Gelu,
                        bias=f1be_c[:, ft:ft + 1],
                        inc=(regd + "e" if fi == 1 else None))
            AEV, BEV = p.val("Ae"), p.val("Be")
            for q in range(4):
                cc = 4 * sI + q
                bi = cc % 2
                r_ = "C" if bi == 0 else "D"
                pr = PREG[r_]
                acquire(r_)
                if q == 0:
                    p.wait("tensor", "Ae", AEV)
                    p.wait("tensor", "Be", BEV)
                    if sI == 0:
                        p.wait("tensor", "ldf2", LDF2)
                for t in range(32):
                    for h2 in range(2):
                        mm(pr[:, h2 * 512:(h2 + 1) * 512],
                           h_fmb[:, t * 512 + q * 128:t * 512 + q * 128 + 128],
                           f2w_s[:, :].rearrange("p (t n) -> p t n", t=32)
                           [:, t, h2 * 512:(h2 + 1) * 512],
                           start=(t == 0), stop=False)
                for h2 in range(2):
                    for oi, osrc in enumerate((xqb16h, xqb16l)):
                        mm(pr[:, h2 * 512:(h2 + 1) * 512], ones128_s[0:1, 0:128],
                           osrc[0:1, h2 * 512:(h2 + 1) * 512], start=False,
                           stop=(oi == 1),
                           inc=(r_ + "f" if (h2 == 1 and oi == 1) else None))
                nfill[r_] += 1
                mark2("f2f", cc, r_ + "f", p.val(r_ + "f"))

        for q in range(4):
            emit_tail(NCH - 4 + q)

    if debug and stage < 5:
        p.wait("gpsimd", "Ce", p.val("Ce"))
        dma(z_dbg, ztm[:, :], inc="dbg")
        if stage >= 2:
            dma(f_dbg, f_all[:, :], inc="dbg")
            dma(e_dbg, e_all[:, :], inc="dbg")
            dma(L_dbg, Lam_all[:, :], inc="dbg")
            dma(g_dbg, g_f32[:, :], inc="dbg")
    if stage >= 5:
        p.wait("gpsimd", "outw0", p.val("outw0"))
        p.wait("gpsimd", "outw1", p.val("outw1"))
    else:
        for nm in ("xnsp0", "xnsp1", "qfsp", "x1sp0", "x1sp1", "dbg", "phq0", "phq1",
                   "osp0", "osp1", "wsp0", "wsp1"):
            p.wait("gpsimd", nm, p.val(nm))
        for nm in ("wqTd", "xqT0", "xqT1", "x1T0", "x1T1", "pkT0", "pkT1"):
            p.wait("sync", nm, p.val(nm))
        p.wait("gpsimd", "v1", p.val("v1"))
        dma(out_h[0:128, :], xbuf[0][:, :], inc="outw")
        p.wait("gpsimd", "outw", p.val("outw"))

    with nc.Block() as block:
        p.finalize(block)
    if stage >= 5:
        p2b.close()
    if stage >= 4:
        if stage < 5:
            p2a.close()
            lamS.close()
    else:
        p1.close()
        lamS.close()
    perm.close()
    return nc, ctx


class _closing:
    def __init__(self, st):
        self.st = st
    def __enter__(self):
        return self
    def __exit__(self, *a):
        self.st.close()



def _consts():
    s = np.arange(C)
    U = (s[:, None] <= s[None, :]).astype(np.float32)
    SL = (s[:, None] > s[None, :]).astype(np.float32)
    return {
        "Ut": U.astype(BF),
        "negUt": (-U).astype(BF),
        "SLt": SL.astype(BF),
        "ones128": np.ones((C, C), BF),
        "eye16": np.eye(16, dtype=np.float32),
        "eye1": np.array([[1.0]], np.float32),
    }


def make_inputs(full, core):
    i = {}
    i["x"] = full["L0_attn_x"][core]
    i["qw"] = full["L0_attn_q_w"]; i["kw"] = full["L0_attn_k_w"]
    i["vw"] = full["L0_attn_v_w"]; i["ow"] = full["L0_attn_o_w"]
    i["decw"] = full["L0_attn_dec_w"]
    i["f1w"] = full["L0_attn_f1_w"]; i["f2w"] = full["L0_attn_f2_w"]
    i["wq"] = full["L0_ffn_weight_q"]
    i["qb"] = full["L0_attn_q_b"][None, :]; i["kb"] = full["L0_attn_k_b"][None, :]
    i["vb"] = full["L0_attn_v_b"][None, :]
    i["decb"] = full["L0_attn_dec_b"][None, :]
    i["f1b"] = full["L0_attn_f1_b"][None, :]
    i["ln1w"] = full["L0_attn_ln1_w"].reshape(8, 128).T
    i["ln1b"] = full["L0_attn_ln1_b"].reshape(8, 128).T
    i["ln2w"] = full["L0_attn_ln2_w"].reshape(8, 128).T
    i["ln2wr"] = full["L0_attn_ln2_w"][None, :]
    i["ln2b"] = full["L0_attn_ln2_b"].reshape(8, 128).T
    i["obt"] = np.tile(full["L0_attn_o_b"][None, :], (128, 1))
    i["f2bt"] = np.tile(full["L0_attn_f2_b"][None, :], (128, 1))
    i["qfbt"] = np.tile(full["L0_ffn_bias"][None, :], (128, 1))
    i["wst"] = np.tile(full["L0_ffn_weight_scale"][None, :], (128, 1))
    i.update(_consts())
    return {k: np.ascontiguousarray(v) for k, v in i.items()}


def kernel(**inputs):
    nc, ctx = build_program(debug=False)
    in_maps = [make_inputs(inputs, core) for core in range(8)]
    res = run_bass_kernel_spmd(nc, in_maps, core_ids=list(range(8)))
    ctx.close()
    return np.stack([np.asarray(r["out"], np.float32) for r in res.results], axis=0)
